# version 22
# baseline (speedup 1.0000x reference)
import numpy as np
import ml_dtypes
from contextlib import ExitStack

import concourse.bass as bass
import concourse.mybir as mybir
from concourse.bass_utils import run_bass_kernel_spmd

F32 = mybir.dt.float32
BF16 = mybir.dt.bfloat16
AF = mybir.ActivationFunctionType
ALU = mybir.AluOpType
BFNP = ml_dtypes.bfloat16

NCORES = 8
T = 1024
TH = 1152
KVH = 8
DM = 4096
FF = 8192
NCH = 8
EPS = 1e-5
SHIFT = -8.0
ISQD = float(1.0 / np.sqrt(128.0))


class Sem:
    def __init__(self, ctx, nc, name):
        self.h = ctx.enter_context(nc.semaphore(name))
        self.n = 0

    def inc(self, instr, k=1):
        instr.then_inc(self.h, k)
        self.n += k
        return instr


_NO_TRACK = {"drain", "wait_ge", "dma_start", "dma_start_transpose", "wait",
             "then_inc", "collective_compute"}
_DW_STATE = {}


def _ap_iv(a):
    try:
        nm = a.name
        off = a.offset
        sh = a.shape
    except Exception:
        return None
    if len(sh) <= 2:
        span = sh[1] if len(sh) == 2 else 1
        return (nm, off, off + span)
    return (nm, 0, 1 << 60)


def _is_ap(a):
    return hasattr(a, "offset") and hasattr(a, "name") and hasattr(a, "shape") \
        and not isinstance(a, (int, float, str))


class _DW:

    def __init__(self, eng, key):
        self._e = eng
        self._k = key
        if key not in _DW_STATE:
            _DW_STATE[key] = ([], [])

    def __getattr__(self, name):
        a = getattr(self._e, name)
        if name in _NO_TRACK or name.startswith("_") or not callable(a):
            return a

        def wrapped(*args, **kw):
            wr, rd = [], []
            for i, arg in enumerate(args):
                if _is_ap(arg):
                    iv = _ap_iv(arg)
                    if iv is not None:
                        (wr if i == 0 else rd).append(iv)
            for kname, arg in kw.items():
                if _is_ap(arg):
                    iv = _ap_iv(arg)
                    if iv is not None:
                        (wr if kname in ("accum_out", "out") else rd).append(iv)
            if not wr and not rd:
                return a(*args, **kw)
            dwr, drd = _DW_STATE[self._k]

            def _hit(ivs, vs):
                for nm, lo, hi in ivs:
                    for nm2, lo2, hi2 in vs:
                        if nm == nm2 and lo < hi2 and lo2 < hi:
                            return True
                return False
            if _hit(wr, dwr) or _hit(rd, dwr) or _hit(wr, drd):
                self._e.drain()
                dwr.clear()
                drd.clear()
            r = a(*args, **kw)
            dwr.extend(wr)
            drd.extend(rd)
            return r
        return wrapped


def build_program():
    _DW_STATE.clear()
    nc = bass.Bass("TRN2", target_bir_lowering=False, dynamic_dma_scratch_size=8192)

    x_seg = nc.dram_tensor("x_seg", [T, DM], F32, kind="ExternalInput")
    x_halo = nc.dram_tensor("x_halo", [128, DM], F32, kind="ExternalInput")
    Wq_sh = nc.dram_tensor("Wq_sh", [512, DM], F32, kind="ExternalInput")
    Wk = nc.dram_tensor("Wk", [DM, 1024], F32, kind="ExternalInput")
    Wv = nc.dram_tensor("Wv", [DM, 1024], F32, kind="ExternalInput")
    Wo_sh = nc.dram_tensor("Wo_sh", [512, DM], F32, kind="ExternalInput")
    w12_sh = nc.dram_tensor("w12_sh", [1024, DM], F32, kind="ExternalInput")
    w3_sh = nc.dram_tensor("w3_sh", [512, DM], F32, kind="ExternalInput")
    b12s = nc.dram_tensor("b12s", [128, 64], F32, kind="ExternalInput")
    b3s = nc.dram_tensor("b3s", [128, 32], F32, kind="ExternalInput")
    cvals_f = nc.dram_tensor("cvals_f", [128, 8], F32, kind="ExternalInput")
    cvals_b = nc.dram_tensor("cvals_b", [128, 4], BF16, kind="ExternalInput")
    ci = {}
    for nm, sh in [("ident", [128, 128]), ("rot", [128, 128]), ("lt", [128, 128]),
                   ("uts", [128, 128]), ("ones", [128, 1]), ("half", [1, 128]),
                   ("onerow", [1, 128]), ("glam", [128, 512]), ("swodd", [128, 256]),
                   ("swevA", [128, 256]), ("swevB", [128, 256]), ("swevA1", [128, 256]),
                   ("cos", [128, TH]), ("sin", [128, TH])]:
        ci[nm] = nc.dram_tensor("ci_" + nm, sh, BF16, kind="ExternalInput")
    ci["eoh"] = nc.dram_tensor("ci_eoh", [128, 8], F32, kind="ExternalInput")

    out = nc.dram_tensor("out", [T, DM], F32, kind="ExternalOutput")

    s_xln1 = nc.dram_tensor("s_xln1", [TH, DM], BF16, kind="Internal")
    s_wkb = nc.dram_tensor("s_wkb", [DM, 1024], BF16, kind="Internal")
    s_wvb = nc.dram_tensor("s_wvb", [DM, 1024], BF16, kind="Internal")
    sh_wq = nc.dram_tensor("sh_wq", [512, DM], BF16, kind="Internal")
    sh_wo = nc.dram_tensor("sh_wo", [512, DM], BF16, kind="Internal")
    sh_w12b = nc.dram_tensor("sh_w12b", [1024, DM], BF16, kind="Internal")
    sh_w3b = nc.dram_tensor("sh_w3b", [512, DM], BF16, kind="Internal")
    sh_w12T = nc.dram_tensor("sh_w12T", [DM, 1024], BF16, kind="Internal")
    sh_w3T = nc.dram_tensor("sh_w3T", [DM, 512], BF16, kind="Internal")
    s_wqb = nc.dram_tensor("s_wqb", [DM, DM], BF16, kind="Internal",
                           addr_space="Shared")
    s_wob = nc.dram_tensor("s_wob", [DM, DM], BF16, kind="Internal",
                           addr_space="Shared")
    s_w12T = nc.dram_tensor("s_w12T", [NCORES * DM, 1024], BF16, kind="Internal",
                            addr_space="Shared")
    s_w3T = nc.dram_tensor("s_w3T", [NCORES * DM, 512], BF16, kind="Internal",
                           addr_space="Shared")
    s_khat = nc.dram_tensor("s_khat", [KVH * 128, TH], BF16, kind="Internal")
    s_ktil = nc.dram_tensor("s_ktil", [KVH * 128, T], BF16, kind="Internal")
    s_o = nc.dram_tensor("s_o", [DM, T], BF16, kind="Internal")
    s_att = nc.dram_tensor("s_att", [DM, T], BF16, kind="Internal")
    s_x2 = nc.dram_tensor("s_x2", [T, DM], F32, kind="Internal")
    s_xln3 = nc.dram_tensor("s_xln3", [T, DM], BF16, kind="Internal")
    s_ffn = nc.dram_tensor("s_ffn", [DM, T], BF16, kind="Internal")
    ag_in = nc.dram_tensor("ag_in", [128, KVH * 129], BF16, kind="Internal")
    ag_out = nc.dram_tensor("ag_out", [NCORES * 128, KVH * 129], BF16, kind="Internal",
                            addr_space="Shared")

    octx = ExitStack()
    with octx:
        octx.enter_context(nc.allow_low_precision(
            "bf16 matmul operands / intermediates, fp32 accumulation (per problem tolerance)"))
        blk = octx.enter_context(nc.Block())

        def V(f):
            blk.vector(lambda v: f(_DW(v, "v")))

        def SC(f):
            blk.scalar(lambda s: f(_DW(s, "s")))

        sT = Sem(octx, nc, "sT")
        sV = Sem(octx, nc, "sV")
        sS = Sem(octx, nc, "sS")

        def sb(ctx, name, shape, dt):
            return ctx.enter_context(nc.sbuf_tensor(name, shape, dt))

        ps0 = octx.enter_context(nc.psum_tensor("ps0", [128, 512], F32))
        ps1 = octx.enter_context(nc.psum_tensor("ps1", [128, 512], F32))
        ps2 = octx.enter_context(nc.psum_tensor("ps2", [128, 512], F32))
        ps3 = octx.enter_context(nc.psum_tensor("ps3", [128, 512], F32))
        psS = octx.enter_context(nc.psum_tensor("psS", [128, 512], F32))
        psZ = octx.enter_context(nc.psum_tensor("psZ", [128, 512], F32))
        psT = octx.enter_context(nc.psum_tensor("psT", [128, 128], BF16))
        psT2 = octx.enter_context(nc.psum_tensor("psT2", [128, 128], BF16))
        PSROT = [ps0, ps1, ps2, ps3]

        C = {nm: sb(octx, "k_" + nm, t.shape, t.dtype) for nm, t in ci.items()
             if nm not in ("cos", "sin")}
        cvf = sb(octx, "k_cvf", [128, 8], F32)
        cvb = sb(octx, "k_cvb", [128, 4], BF16)
        for dt_, tile, vals in [
            (F32, cvf, [EPS, SHIFT, ISQD, -1.0, 1.0 / DM, 0.5, 1.0, 0.0]),
            (BF16, cvb, [SHIFT, -1.0, 1.0, 0.0]),
        ]:
            for col, val in enumerate(vals):
                try:
                    nc.const_aps.register(dt_, float(val), tile[:, col:col + 1])
                except Exception:
                    pass
        b12sb = sb(octx, "k_b12", [128, 64], F32)
        b3sb = sb(octx, "k_b3", [128, 32], F32)
        stt = [sb(octx, "k_st0", [128, 8], F32), sb(octx, "k_st1", [128, 8], F32)]
        s_cp = {nm: Sem(octx, nc, "scp_" + nm) for nm in ["wk", "wv", "wq", "wo", "w12", "w3"]}
        ag_w = {nm: Sem(octx, nc, "agw_" + nm) for nm in ["wq", "wo", "w12", "w3"]}
        wT_buf = [sb(octx, "wTb0", [128, 512], BF16), sb(octx, "wTb1", [128, 512], BF16)]
        s_wTt = [Sem(octx, nc, "wTt0"), Sem(octx, nc, "wTt1")]
        s_wTw = [Sem(octx, nc, "wTw0"), Sem(octx, nc, "wTw1")]

        sem_c0 = Sem(octx, nc, "c0")

        def ph0_consts(s):
            for nm in C:
                pass
            for nm in [n for n in ci if n not in ("cos", "sin")]:
                sem_c0.inc(s.dma_start(out=C[nm][:, :], in_=ci[nm][:, :]), 16)
            sem_c0.inc(s.dma_start(out=b12sb[:, :], in_=b12s[:, :]), 16)
            sem_c0.inc(s.dma_start(out=b3sb[:, :], in_=b3s[:, :]), 16)
            sem_c0.inc(s.dma_start(out=cvf[:, :], in_=cvals_f[:, :]), 16)
            sem_c0.inc(s.dma_start(out=cvb[:, :], in_=cvals_b[:, :]), 16)
            s.wait_ge(sem_c0.h, sem_c0.n)
        blk.sync(ph0_consts)
        for _eng in ("tensor", "vector", "scalar", "gpsimd"):
            getattr(blk, _eng)(lambda e: e.wait_ge(sem_c0.h, sem_c0.n))

        def w_copies(g):
            for nm, srcT, dstT in [("wk", Wk, s_wkb), ("wv", Wv, s_wvb)]:
                s_cp[nm].inc(g.dma_start(out=dstT[:, :], in_=srcT[:, :]), 16)
        blk.gpsimd(w_copies)

        def w_ags_pre(g):
            g.wait_ge(s_cp["wq"].h, 16)
            ag_w["wq"].inc(g.collective_compute(
                "AllGather", ALU.bypass, replica_groups=[list(range(NCORES))],
                ins=[sh_wq[:, :]], outs=[s_wqb[:, :]]), 1)
            g.wait_ge(s_cp["wo"].h, 16)
            ag_w["wo"].inc(g.collective_compute(
                "AllGather", ALU.bypass, replica_groups=[list(range(NCORES))],
                ins=[sh_wo[:, :]], outs=[s_wob[:, :]]), 1)

        _wT_jobs = []
        for k0 in range(0, DM, 128):
            for m0 in range(0, 1024, 512):
                _wT_jobs.append((sh_w12b, sh_w12T, m0, k0))
        for k0 in range(0, DM, 128):
            _wT_jobs.append((sh_w3b, sh_w3T, 0, k0))
        _wT_state = {"i": 0}

        def emit_w_chunks(n):
            i0 = _wT_state["i"]
            if i0 >= len(_wT_jobs):
                return
            i1 = min(i0 + n, len(_wT_jobs))
            _wT_state["i"] = i1

            def go(s, i0=i0, i1=i1):
                for i in range(i0, i1):
                    srcb, dstT, m0, k0 = _wT_jobs[i]
                    p = i % 2
                    if i < 2:
                        s.wait_ge(s_cp["w12"].h, 16)
                        s.wait_ge(s_cp["w3"].h, 16)
                    else:
                        s.wait_ge(s_wTw[p].h, s_wTw[p].n)
                    s_wTt[p].inc(s.dma_start(out=wT_buf[p][:, :],
                                             in_=srcb[m0:m0 + 512, k0:k0 + 128],
                                             transpose=True), 16)
                    s.wait_ge(s_wTt[p].h, s_wTt[p].n)
                    s_wTw[p].inc(s.dma_start(out=dstT[k0:k0 + 128, m0:m0 + 512],
                                             in_=wT_buf[p][:, :]), 16)
            blk.sync(go)

        s1 = ExitStack()
        with s1:
            xln1T = sb(s1, "xln1T", [128, 32 * TH], BF16)
            C["cos"] = sb(s1, "k_cos", [128, TH], BF16)
            C["sin"] = sb(s1, "k_sin", [128, TH], BF16)

            def ph0b(s):
                sem_c0.inc(s.dma_start(out=C["cos"][:, :], in_=ci["cos"][:, :]), 16)
                sem_c0.inc(s.dma_start(out=C["sin"][:, :], in_=ci["sin"][:, :]), 16)
                s.wait_ge(sem_c0.h, sem_c0.n)
            blk.sync(ph0b)

            def xt(kc):
                return xln1T[:, kc * TH:(kc + 1) * TH]

            wkv_sb = [[sb(s1, "wkv00", [128, DM], BF16), None, None],
                      [sb(s1, "wkv10", [128, DM], BF16), None, None]]
            g_wkv = [[Sem(octx, nc, f"gwkv0{j}") for j in range(3)],
                     [Sem(octx, nc, f"gwkv1{j}") for j in range(3)]]
            kv_free = {}
            g_tot = {}

            def emit_kv_load(hl):
                for w_i, wsrc in ((0, s_wkb), (1, s_wvb)):
                    stq = kv_free.get((w_i, hl % 3), 0)

                    def a1_w(s, hl=hl, w_i=w_i, wsrc=wsrc, stq=stq):
                        if hl <= 1:
                            s.wait_ge(s_cp["wk" if w_i == 0 else "wv"].h, 16)
                        if stq:
                            s.wait_ge(sT.h, stq)
                        for kcg in range(2):
                            g_wkv[w_i][hl % 3].inc(s.dma_start(
                                out=wkv_sb[w_i][hl % 3][:, kcg * 2048:(kcg + 1) * 2048]
                                .rearrange("p (kc j) -> p kc j", j=128),
                                in_=wsrc[kcg * 2048:(kcg + 1) * 2048, hl * 128:(hl + 1) * 128]
                                .rearrange("(kc p) j -> p kc j", p=128)), 16)
                    blk.scalar(a1_w)
                    g_tot[(w_i, hl)] = g_wkv[w_i][hl % 3].n

            s1a = ExitStack()
            s1a.__enter__()
            trash = sb(s1a, "p1trash", [128, DM], BF16)
            p1x = [sb(s1a, "p1x0", [128, DM], F32), sb(s1a, "p1x1", [128, DM], F32)]
            p1l = [sb(s1a, "p1l0", [128, DM], BF16), sb(s1a, "p1l1", [128, DM], BF16)]
            lx = [Sem(octx, nc, "lx0"), Sem(octx, nc, "lx1")]
            wx = Sem(octx, nc, "wx")
            tr_g = [Sem(octx, nc, f"tr{i}") for i in range(4)]

            for tc in range(9):
                p = tc % 2
                src = x_halo if tc == 0 else x_seg
                r0 = 0 if tc == 0 else (tc - 1) * 128
                svq = sV.n

                def p1_load(s, src=src, r0=r0, p=p, tc=tc, svq=svq):
                    if tc >= 2:
                        s.wait_ge(sV.h, svq)
                    lx[p].inc(s.dma_start(out=p1x[p][:, :], in_=src[r0:r0 + 128, :]), 16)
                blk.sync(p1_load)

                def p1_sv(v, p=p):
                    v.wait_ge(lx[p].h, lx[p].n)
                    sV.inc(v.reduce_sum(stt[p][:, 0:1], p1x[p][:, :], axis=mybir.AxisListType.X))
                V(p1_sv)

                def p1_ss(s, p=p):
                    s.wait_ge(lx[p].h, lx[p].n)
                    sS.inc(s.activation(trash[:, :], p1x[p][:, :], AF.Square, accum_out=stt[p][:, 1:2]))
                SC(p1_ss)
                svn, ssn = sV.n, sS.n

                def p1_mk(v, p=p, svn=svn, ssn=ssn):
                    v.wait_ge(sV.h, svn)
                    v.wait_ge(sS.h, ssn)
                    st = stt[p]
                    v.tensor_scalar(st[:, 2:3], st[:, 0:1], 1.0 / DM, None, ALU.mult)
                    v.tensor_scalar(st[:, 3:4], st[:, 1:2], 1.0 / DM, None, ALU.mult)
                    v.tensor_tensor(st[:, 4:5], st[:, 2:3], st[:, 2:3], ALU.mult)
                    sV.inc(v.tensor_tensor(st[:, 3:4], st[:, 3:4], st[:, 4:5], ALU.subtract))
                V(p1_mk)
                svn = sV.n

                def p1_sq(s, p=p, svn=svn):
                    s.wait_ge(sV.h, svn)
                    sS.inc(s.activation(stt[p][:, 4:5], stt[p][:, 3:4], AF.Sqrt, bias=EPS))
                SC(p1_sq)
                ssn = sS.n

                wxq = wx.n

                def p1_fin(v, p=p, ssn=ssn, wxq=wxq):
                    v.wait_ge(sS.h, ssn)
                    v.wait_ge(wx.h, wxq)
                    st = stt[p]
                    v.reciprocal(st[:, 5:6], st[:, 4:5])
                    v.tensor_tensor(st[:, 6:7], st[:, 2:3], st[:, 5:6], ALU.mult)
                    v.tensor_scalar(st[:, 6:7], st[:, 6:7], -1.0, None, ALU.mult)
                    sV.inc(v.tensor_scalar(p1l[p][:, :], p1x[p][:, :],
                                           st[:, 5:6], st[:, 6:7], ALU.mult, ALU.add))
                V(p1_fin)
                svn = sV.n

                def p1_st(s, tc=tc, p=p, svn=svn):
                    s.wait_ge(sV.h, svn)
                    wx.inc(s.dma_start(out=s_xln1[tc * 128:(tc + 1) * 128, :], in_=p1l[p][:, :]), 16)
                SC(p1_st)

            emit_kv_load(0)

            def p1_tr(s):
                s.wait_ge(wx.h, wx.n)
                for kc in range(32):
                    tr_g[kc // 8].inc(s.dma_start(out=xt(kc), in_=s_xln1[:, kc * 128:(kc + 1) * 128],
                                                  transpose=True), 16)
            blk.sync(p1_tr)

            tr_tot = [g.n for g in tr_g]

            def w_copies_late(g, tr_tot=tr_tot):
                for gg, tt in zip(tr_g, tr_tot):
                    g.wait_ge(gg.h, tt)
                for nm, srcT, dstT in [("w12", w12_sh, sh_w12b), ("w3", w3_sh, sh_w3b),
                                       ("wq", Wq_sh, sh_wq), ("wo", Wo_sh, sh_wo)]:
                    s_cp[nm].inc(g.dma_start(out=dstT[:, :], in_=srcT[:, :]), 16)
            blk.gpsimd(w_copies_late)

            def wait_xln1T(eng):
                for g in tr_g:
                    eng.wait_ge(g.h, g.n)

            s1a.__exit__(None, None, None)
            v_tok = sb(s1, "v_tok", [128, 9 * 1024], BF16)
            S_entry = sb(s1, "S_entry", [128, KVH * NCH * 128], BF16)
            expA = sb(s1, "expA", [128, KVH * NCH * 128], BF16)
            Dcum = sb(s1, "Dcum", [128, KVH * NCH], F32)
            S_run = sb(s1, "S_run", [128, 128], F32)
            logD = sb(s1, "logD", [128, 1], F32)
            S_in = sb(s1, "S_in", [128, KVH * 128], BF16)
            s1b = ExitStack()
            s1b.__enter__()
            k_feat = sb(s1b, "k_feat", [128, KVH * TH], BF16)
            sigk = sb(s1b, "sigk", [128, TH], BF16)
            v_feat = sb(s1b, "v_feat", [128, TH], BF16)
            gs_bf = [sb(s1b, "gs0", [128, 128], BF16), sb(s1b, "gs1", [128, 128], BF16)]
            gs_tok = [sb(s1b, "gst0", [128, 128], BF16), sb(s1b, "gst1", [128, 128], BF16)]
            ek_bf = sb(s1b, "ek", [128, 128], BF16)
            zkr = sb(s1b, "zkr", [1, 128], BF16)
            zkb = sb(s1b, "zkb", [128, 128], BF16)
            t1_bf = sb(s1b, "t1", [128, 128], BF16)
            exn_bf = sb(s1b, "exn", [128, 128], BF16)
            exb_bf = sb(s1b, "exb", [128, 128], BF16)
            kbar_bf = sb(s1b, "kbar", [128, 128], BF16)
            kbar_tok = sb(s1b, "kbart", [128, 128], BF16)
            dc_f32 = sb(s1b, "dcf", [128, 1], F32)
            khat = sb(s1b, "khat", [128, TH], BF16)
            ktil = sb(s1b, "ktil", [128, T], BF16)
            xr_bf = sb(s1b, "xr", [128, 384], BF16)
            sexit_bf = sb(s1b, "sexit", [128, 128], BF16)
            dseg_bf = sb(s1b, "dseg", [128, 1], BF16)
            wkk = Sem(octx, nc, "wkk")
            sem_ag = Sem(octx, nc, "agst")
            sem_cc = Sem(octx, nc, "cc")

            def kf(h):
                return k_feat[:, h * TH:(h + 1) * TH]

            def sg(h):
                return sigk

            def vt(ch):
                return v_tok[:, ch * 1024:(ch + 1) * 1024]

            def se(h, c):
                return S_entry[:, (h * NCH + c) * 128:(h * NCH + c + 1) * 128]

            def ea(h, c):
                return expA[:, (h * NCH + c) * 128:(h * NCH + c + 1) * 128]

            wkv_sb[0][1] = sb(s1b, "wkv01", [128, DM], BF16)
            wkv_sb[1][1] = sb(s1b, "wkv11", [128, DM], BF16)
            wkv_sb[0][2] = sb(s1b, "wkv02", [128, DM], BF16)
            wkv_sb[1][2] = sb(s1b, "wkv12", [128, DM], BF16)
            emit_kv_load(1)
            emit_kv_load(2)
            for h in range(KVH):
                for w_i in range(2):
                    for tt in range(3):
                        myps = PSROT[2 * w_i + tt % 2]
                        svq = sV.n
                        gq = g_tot[(w_i, h)]

                        def a1_mm(t, h=h, w_i=w_i, tt=tt, myps=myps, svq=svq, gq=gq):
                            t.wait_ge(g_wkv[w_i][h % 3].h, gq)
                            wait_xln1T(t)
                            t.wait_ge(sV.h, svq)
                            for kc in range(32):
                                ins = t.matmul(myps[:, 0:384],
                                               wkv_sb[w_i][h % 3][:, kc * 128:(kc + 1) * 128],
                                               xt(kc)[:, tt * 384:(tt + 1) * 384],
                                               start=(kc == 0), stop=(kc == 31))
                                if kc == 31:
                                    sT.inc(ins)
                        blk.tensor(a1_mm)
                        stn = sT.n

                        def a1_ev(v, h=h, w_i=w_i, tt=tt, myps=myps, stn=stn):
                            v.wait_ge(sT.h, stn)
                            dst = kf(h) if w_i == 0 else v_feat
                            sV.inc(v.tensor_copy(dst[:, tt * 384:(tt + 1) * 384], myps[:, 0:384]))
                        V(a1_ev)
                    kv_free[(w_i, h % 3)] = sT.n
                if h + 3 < KVH:
                    emit_kv_load(h + 3)

                for ch in range(9):
                    myt = psT if ch % 2 == 0 else psT2
                    svn = sV.n

                    def a1_tr(t, ch=ch, myt=myt, svn=svn):
                        t.wait_ge(sV.h, svn)
                        sT.inc(t.transpose(myt[:, :], v_feat[:, ch * 128:(ch + 1) * 128], C["ident"][:, :]))
                    blk.tensor(a1_tr)
                    stn = sT.n

                    def a1_te(v, ch=ch, h=h, myt=myt, stn=stn):
                        v.wait_ge(sT.h, stn)
                        sV.inc(v.tensor_copy(vt(ch)[:, h * 128:(h + 1) * 128], myt[:, :]))
                    V(a1_te)
                emit_w_chunks(8)

            for h in range(KVH):
                hp = h % 2
                svq_sig = sV.n

                def a2_sig(s, h=h, svq_sig=svq_sig):
                    s.wait_ge(sV.h, svq_sig)
                    sS.inc(s.activation(sigk[:, :], kf(h)[:, :], AF.Sigmoid))
                SC(a2_sig)

                ssq_i = sS.n

                def a3_init(v, h=h, ssq_i=ssq_i):
                    v.wait_ge(sS.h, ssq_i)
                    v.memset(S_run[:, :], 0.0)
                    v.memset(logD[:, :], 0.0)
                    sV.inc(v.memset(se(h, 0)[:, :], 0.0))
                V(a3_init)

                for tt in range(3):
                    svn = sV.n

                    def a3_rot(t, h=h, tt=tt, svn=svn):
                        t.wait_ge(sV.h, svn)
                        if tt == 0:
                            t.wait_ge(wkk.h, wkk.n)
                        sT.inc(t.matmul(psS[:, 0:384], C["rot"][:, :],
                                        kf(h)[:, tt * 384:(tt + 1) * 384], start=True, stop=True))
                    blk.tensor(a3_rot)
                    stn = sT.n

                    def a3_rope(v, h=h, hp=hp, tt=tt, stn=stn):
                        v.wait_ge(sT.h, stn)
                        sl = slice(tt * 384, (tt + 1) * 384)
                        v.tensor_copy(xr_bf[:, :], psS[:, 0:384])
                        v.tensor_tensor(khat[:, sl], kf(h)[:, sl], C["cos"][:, sl], ALU.mult)
                        v.tensor_tensor(xr_bf[:, :], xr_bf[:, :], C["sin"][:, sl], ALU.mult)
                        sV.inc(v.tensor_tensor(khat[:, sl], khat[:, sl], xr_bf[:, :], ALU.add))
                    V(a3_rope)

                for c in range(NCH):
                    cp = c % 2
                    myt = psT if cp == 0 else psT2
                    myt2 = psT2 if cp == 0 else psT
                    svn = sV.n

                    def a3_ln(s, h=h, c=c, cp=cp, svn=svn):
                        s.wait_ge(sV.h, svn)
                        sS.inc(s.activation(gs_bf[cp][:, :],
                                            sg(h)[:, 128 + c * 128:128 + (c + 1) * 128], AF.Ln))
                    SC(a3_ln)
                    ssn = sS.n

                    def a3_gst(t, cp=cp, myt=myt, ssn=ssn):
                        t.wait_ge(sS.h, ssn)
                        sT.inc(t.transpose(myt[:, :], gs_bf[cp][:, :], C["ident"][:, :]))
                    blk.tensor(a3_gst)
                    stn = sT.n

                    def a3_gse(v, cp=cp, myt=myt, stn=stn):
                        v.wait_ge(sT.h, stn)
                        sV.inc(v.tensor_copy(gs_tok[cp][:, :], myt[:, :]))
                    V(a3_gse)
                    svn = sV.n

                    def a3_amm(t, cp=cp, svn=svn, ssn=ssn):
                        t.wait_ge(sV.h, svn)
                        t.wait_ge(sS.h, sS.n)
                        t.matmul(psS[:, 0:128], gs_tok[cp][:, :], C["lt"][:, :], start=True, stop=True)
                        sT.inc(t.matmul(psS[:, 128:256], gs_tok[cp][:, :], C["uts"][:, :],
                                        start=True, stop=True))
                    blk.tensor(a3_amm)
                    stn = sT.n

                    def a3_ex(s, h=h, c=c, stn=stn):
                        s.wait_ge(sT.h, stn)
                        s.wait_ge(sV.h, sV.n)
                        s.activation(Dcum[:, h * NCH + c:h * NCH + c + 1], logD[:, :], AF.Exp)
                        s.activation(ea(h, c)[:, :], psS[:, 0:128], AF.Exp)
                        s.activation(dc_f32[:, :], psS[:, 127:128], AF.Exp)
                        s.activation(exn_bf[:, :], psS[:, 0:128], AF.Exp, scale=-1.0)
                        s.activation(exb_bf[:, :], psS[:, 128:256], AF.Exp)
                        sS.inc(s.activation(ek_bf[:, :], kf(h)[:, 128 + c * 128:128 + (c + 1) * 128],
                                            AF.Exp, bias=SHIFT))
                    SC(a3_ex)
                    ssn = sS.n

                    def a3_ld(v, ssn=ssn):
                        v.wait_ge(sS.h, ssn)
                        sV.inc(v.tensor_tensor(logD[:, :], logD[:, :], psS[:, 127:128], ALU.add))
                    V(a3_ld)
                    svn = sV.n

                    def a3_zk(t, ssn=ssn, svn=svn):
                        t.wait_ge(sS.h, ssn)
                        t.wait_ge(sV.h, svn)
                        sT.inc(t.matmul(psZ[0:1, 0:128], C["ones"][:, :], ek_bf[:, :],
                                        start=True, stop=True))
                    blk.tensor(a3_zk)
                    stn = sT.n

                    def a3_zr(v, stn=stn):
                        v.wait_ge(sT.h, stn)
                        sV.inc(v.reciprocal(zkr[:, :], psZ[0:1, 0:128]))
                    V(a3_zr)
                    svn = sV.n

                    def a3_zb(t, svn=svn):
                        t.wait_ge(sV.h, svn)
                        sT.inc(t.matmul(psZ[:, 128:256], C["onerow"][:, :], zkr[:, :],
                                        start=True, stop=True))
                    blk.tensor(a3_zb)
                    stn = sT.n

                    def a3_kt(v, h=h, hp=hp, c=c, stn=stn):
                        v.wait_ge(sT.h, stn)
                        v.tensor_copy(zkb[:, :], psZ[:, 128:256])
                        v.tensor_tensor(t1_bf[:, :], ek_bf[:, :], zkb[:, :], ALU.mult)
                        v.tensor_tensor(ktil[:, c * 128:(c + 1) * 128], t1_bf[:, :],
                                        exn_bf[:, :], ALU.mult)
                        sV.inc(v.tensor_tensor(kbar_bf[:, :], t1_bf[:, :], exb_bf[:, :], ALU.mult))
                    V(a3_kt)
                    svn = sV.n

                    def a3_kbt(t, myt2=myt2, svn=svn):
                        t.wait_ge(sV.h, svn)
                        sT.inc(t.transpose(myt2[:, :], kbar_bf[:, :], C["ident"][:, :]))
                    blk.tensor(a3_kbt)
                    stn = sT.n

                    def a3_kbe(v, myt2=myt2, stn=stn):
                        v.wait_ge(sT.h, stn)
                        sV.inc(v.tensor_copy(kbar_tok[:, :], myt2[:, :]))
                    V(a3_kbe)
                    svn = sV.n

                    def a3_kv(t, h=h, c=c, svn=svn):
                        t.wait_ge(sV.h, svn)
                        sT.inc(t.matmul(psS[:, 256:384], kbar_tok[:, :],
                                        vt(c + 1)[:, h * 128:(h + 1) * 128], start=True, stop=True))
                    blk.tensor(a3_kv)
                    stn = sT.n

                    agq = sem_ag.n

                    def a3_su(v, h=h, c=c, stn=stn, ssn=ssn, agq=agq):
                        v.wait_ge(sT.h, stn)
                        v.wait_ge(sS.h, ssn)
                        if c == NCH - 1:
                            v.wait_ge(sem_ag.h, agq)
                        v.scalar_tensor_tensor(S_run[:, :], S_run[:, :], dc_f32[:, 0:1],
                                               psS[:, 256:384], ALU.mult, ALU.add)
                        dst = se(h, c + 1) if c < NCH - 1 else sexit_bf
                        sV.inc(v.tensor_copy(dst[:, :], S_run[:, :]))
                    V(a3_su)

                svn, ssn0 = sV.n, sS.n

                agq2 = sem_ag.n

                def a3_ds(s, svn=svn, agq2=agq2):
                    s.wait_ge(sV.h, svn)
                    s.wait_ge(sem_ag.h, agq2)
                    sS.inc(s.activation(dseg_bf[:, :], logD[:, :], AF.Exp))
                SC(a3_ds)
                ssn = sS.n

                def a3_ag(s, h=h, hp=hp, ssn=ssn, svn=svn):
                    s.wait_ge(sS.h, ssn)
                    s.wait_ge(sV.h, svn)
                    sem_ag.inc(s.dma_start(out=ag_in[:, h * 129:h * 129 + 128], in_=sexit_bf[:, :]), 16)
                    sem_ag.inc(s.dma_start(out=ag_in[:, h * 129 + 128:h * 129 + 129],
                                           in_=dseg_bf[:, :],
                                           allow_slow_non_contiguous=True), 16)
                    wkk.inc(s.dma_start(out=s_khat[h * 128:(h + 1) * 128, :], in_=khat[:, :]), 16)
                    wkk.inc(s.dma_start(out=s_ktil[h * 128:(h + 1) * 128, :], in_=ktil[:, :]), 16)
                SC(a3_ag)
                emit_w_chunks(4)
                if h == 5:
                    blk.gpsimd(w_ags_pre)

            def a4(g):
                g.wait_ge(sem_ag.h, sem_ag.n)
                sem_cc.inc(g.collective_compute(
                    "AllGather", ALU.bypass, replica_groups=[list(range(NCORES))],
                    ins=[ag_in[:, :]], outs=[ag_out[:, :]]), 1)
            blk.gpsimd(a4)

            assert _wT_state["i"] == len(_wT_jobs)
            wTw_tot = (s_wTw[0].n, s_wTw[1].n)

            def w_ags_post(g):
                g.wait_ge(s_wTw[0].h, wTw_tot[0])
                g.wait_ge(s_wTw[1].h, wTw_tot[1])
                ag_w["w12"].inc(g.collective_compute(
                    "AllGather", ALU.bypass, replica_groups=[list(range(NCORES))],
                    ins=[sh_w12T[:, :]], outs=[s_w12T[:, :]]), 1)
                ag_w["w3"].inc(g.collective_compute(
                    "AllGather", ALU.bypass, replica_groups=[list(range(NCORES))],
                    ins=[sh_w3T[:, :]], outs=[s_w3T[:, :]]), 1)
            blk.gpsimd(w_ags_post)

            s1b.__exit__(None, None, None)
            wkk_tot, ag_tot = wkk.n, sem_ag.n
            for _f in (
                lambda e: (e.wait_ge(wkk.h, wkk_tot), e.wait_ge(sem_ag.h, ag_tot)),
            ):
                V(_f)
                SC(_f)
                blk.sync(_f)
                blk.gpsimd(_f)
            s1c = ExitStack()
            s1c.__enter__()
            wq_sb = [[sb(s1c, "wq00", [128, 2048], BF16), sb(s1c, "wq01", [128, 2048], BF16)],
                     [sb(s1c, "wq10", [128, 2048], BF16), sb(s1c, "wq11", [128, 2048], BF16)]]
            g_wq = [[Sem(octx, nc, f"gwq0{j}") for j in range(2)],
                    [Sem(octx, nc, f"gwq1{j}") for j in range(2)]]
            l_kk = [Sem(octx, nc, "lkk0"), Sem(octx, nc, "lkk1")]
            khat_r = [sb(s1c, "khr0", [128, TH], BF16)]
            ktil_r = [sb(s1c, "ktr0", [128, T], BF16)]

            def pb_pre(s):
                s.wait_ge(wkk.h, wkk.n)
                l_kk[0].inc(s.dma_start(out=khat_r[0][:, :], in_=s_khat[0:128, :]), 16)
                l_kk[0].inc(s.dma_start(out=ktil_r[0][:, :], in_=s_ktil[0:128, :]), 16)
                s.wait_ge(ag_w["wq"].h, 1)
                for mc in range(2):
                    for kh in range(2):
                        g_wq[mc][kh].inc(s.dma_start(
                            out=wq_sb[mc][kh][:, :]
                            .rearrange("p (kc j) -> p kc j", j=128),
                            in_=s_wqb[kh * 2048:(kh + 1) * 2048, mc * 128:(mc + 1) * 128]
                            .rearrange("(kc p) j -> p kc j", p=128)), 16)
            blk.sync(pb_pre)
            tmp_pc = sb(s1c, "tmp_pc", [128, 128], BF16)
            exp_q = sb(s1c, "exp_q", [128, T], BF16)
            q_bf = sb(s1c, "q_bf", [128, 512], BF16)
            xrq_bf = sb(s1c, "xrq", [128, 512], BF16)
            qhat = sb(s1c, "qhat", [128, 4 * T], BF16)
            qtil = sb(s1c, "qtil", [128, 4 * T], BF16)
            zqr = sb(s1c, "zqr", [1, 512], BF16)
            zqb = sb(s1c, "zqb", [128, 512], BF16)
            qt_tmp = sb(s1c, "qt_tmp", [128, 512], BF16)
            s1c2 = ExitStack()
            s1c2.__enter__()
            dsg8 = sb(s1c2, "dsg8", [128, 8], F32)
            P_all = sb(s1c2, "P_all", [128, KVH * 128], BF16)
            tmp_pc8 = sb(s1c2, "tmp_pc8", [128, KVH * 128], BF16)
            ag_all = sb(s1c2, "ag_all", [128, 4 * 1032], BF16)
            lag = [Sem(octx, nc, "lag0"), Sem(octx, nc, "lag1")]

            sv_pc = [0]
            for gh in range(2):
                svq_pc = sv_pc[0]

                def pc_load(s, gh=gh, svq_pc=svq_pc):
                    if gh == 0:
                        s.wait_ge(sem_cc.h, sem_cc.n)
                    else:
                        s.wait_ge(sV.h, svq_pc)
                    lag[gh].inc(s.dma_start(
                        out=ag_all[:, :].rearrange("p (r c) -> p r c", c=1032),
                        in_=ag_out[gh * 512:(gh + 1) * 512, :]
                        .rearrange("(r p) c -> p r c", p=128)), 16)
                blk.sync(pc_load)

                def pc_comb(v, gh=gh):
                    v.wait_ge(lag[gh].h, lag[gh].n)
                    for rr in range(4):
                        r = gh * 4 + rr
                        agr = ag_all[:, rr * 1032:(rr + 1) * 1032]
                        if r == 0:
                            v.memset(P_all[:, :], 0.0)
                            v.memset(S_in[:, :], 0.0)
                        if r == 4:
                            v.memset(P_all[:, :], 0.0)
                        v.tensor_scalar(tmp_pc8[:, :], P_all[:, :], C["eoh"][:, r:r + 1],
                                        None, ALU.mult)
                        v.tensor_tensor(S_in[:, :], S_in[:, :], tmp_pc8[:, :], ALU.add)
                        v.tensor_copy(dsg8[:, :],
                                      agr.rearrange("p (h c) -> p h c", c=129)[:, :, 128:129])
                        for h in range(KVH):
                            v.scalar_tensor_tensor(P_all[:, h * 128:(h + 1) * 128],
                                                   P_all[:, h * 128:(h + 1) * 128],
                                                   dsg8[:, h:h + 1],
                                                   agr[:, h * 129:h * 129 + 128],
                                                   ALU.mult, ALU.add)
                    sV.inc(v.memset(tmp_pc[:, 0:1], 0.0))
                V(pc_comb)
                sv_pc[0] = sV.n
            s1c2.__exit__(None, None, None)

            w_os = Sem(octx, nc, "w_os")
            eT = [sb(s1c, f"eT{i}", [128, 256], BF16) for i in range(3)]
            zswr = sb(s1c, "zswr", [1, 512], BF16)
            zswb = sb(s1c, "zswb", [128, 512], F32)
            pt_bf = sb(s1c, "pt_bf", [128, 512], BF16)
            tswa = sb(s1c, "tswa", [128, 512], F32)
            o_buf = sb(s1c, "o_buf", [128, 4 * T], BF16)
            wq_free = {}
            g_tot_wq = {(0, 0): 16, (0, 1): 32, (1, 0): 16, (1, 1): 32}
            kk_free = [0, 0]
            ps_free = {}
            qt_end = [0]

            def qhat_ap(j):
                return qhat[:, :].rearrange("p (h t) -> p h t", h=4)[:, :, j * 64:(j + 1) * 64]

            def qtil_ap(c):
                return qtil[:, :].rearrange("p (h j i) -> p j h i", h=4, j=16)[:, 2 * c:2 * c + 2, :, :]

            for h in range(KVH):
                hp = h % 2

                kkq = kk_free[0]

                def b_kk(s, h=h, hp=hp, kkq=kkq):
                    s.wait_ge(sT.h, kkq)
                    l_kk[0].inc(s.dma_start(out=khat_r[0][:, :],
                                             in_=s_khat[h * 128:(h + 1) * 128, :]), 16)
                    l_kk[0].inc(s.dma_start(out=ktil_r[0][:, :],
                                             in_=s_ktil[h * 128:(h + 1) * 128, :]), 16)
                if h >= 1:
                    blk.sync(b_kk)

                for mc in range(4):
                    myps = [PSROT[(2 * mc + tt) % 4] for tt in range(2)]
                    gm = h * 4 + mc
                    pair = gm % 2
                    for kh in range(2):
                        stq = wq_free.get(pair, 0)

                        def b_wq(s, h=h, mc=mc, kh=kh, pair=pair, stq=stq):
                            if stq:
                                s.wait_ge(sT.h, stq)
                            g_wq[pair][kh].inc(s.dma_start(
                                out=wq_sb[pair][kh][:, :]
                                .rearrange("p (kc j) -> p kc j", j=128),
                                in_=s_wqb[kh * 2048:(kh + 1) * 2048,
                                          (4 * h + mc) * 128:(4 * h + mc + 1) * 128]
                                .rearrange("(kc p) j -> p kc j", p=128)), 16)
                        if gm >= 2:
                            blk.sync(b_wq)
                        g_tot_wq[(gm, kh)] = g_wq[pair][kh].n

                    stn_done = {}
                    for kh in range(2):
                        for tt in range(2):
                            idx = (2 * mc + tt) % 4
                            fr_s, fr_v = ps_free.get(idx, (0, 0))
                            gq = g_tot_wq[(gm, kh)]

                            def b_qmm(t, kh=kh, tt=tt, pair=pair, myps=myps,
                                      fr_s=fr_s, fr_v=fr_v, gq=gq):
                                t.wait_ge(g_wq[pair][kh].h, gq)
                                if kh == 0:
                                    t.wait_ge(sS.h, fr_s)
                                    t.wait_ge(sV.h, fr_v)
                                for kc in range(16):
                                    ins = t.matmul(myps[tt][:, :],
                                                   wq_sb[pair][kh][:, kc * 128:(kc + 1) * 128],
                                                   xt(kh * 16 + kc)[:, 128 + tt * 512:128 + (tt + 1) * 512],
                                                   start=(kh == 0 and kc == 0),
                                                   stop=(kh == 1 and kc == 15))
                                    if kc == 15:
                                        sT.inc(ins)
                            blk.tensor(b_qmm)
                            if kh == 1:
                                stn_done[tt] = sT.n
                        wq_free[pair] = sT.n

                    for tt in range(2):
                        stn = stn_done[tt]
                        ps_tt = myps[tt]
                        qtp = qt_end[0]

                        def b_qe(s, tt=tt, ps_tt=ps_tt, stn=stn, qtp=qtp):
                            s.wait_ge(sT.h, stn)
                            s.wait_ge(sV.h, qtp)
                            sS.inc(s.activation(exp_q[:, tt * 512:(tt + 1) * 512],
                                                ps_tt[:, :], AF.Exp, bias=SHIFT))
                        SC(b_qe)

                        def b_qb(v, ps_tt=ps_tt, stn=stn):
                            v.wait_ge(sT.h, stn)
                            sV.inc(v.tensor_copy(q_bf[:, :], ps_tt[:, :]))
                        V(b_qb)
                        ps_free[(2 * mc + tt) % 4] = (sS.n, sV.n)
                        svn = sV.n

                        def b_qr(t, svn=svn):
                            t.wait_ge(sV.h, svn)
                            sT.inc(t.matmul(psS[:, :], C["rot"][:, :], q_bf[:, :], start=True, stop=True))
                        blk.tensor(b_qr)
                        stn2 = sT.n

                        def b_qro(v, mc=mc, tt=tt, stn2=stn2):
                            v.wait_ge(sT.h, stn2)
                            sl = slice(128 + tt * 512, 128 + (tt + 1) * 512)
                            dsl = slice(mc * T + tt * 512, mc * T + (tt + 1) * 512)
                            v.tensor_copy(xrq_bf[:, :], psS[:, :])
                            v.tensor_tensor(qhat[:, dsl], q_bf[:, :], C["cos"][:, sl], ALU.mult)
                            v.tensor_tensor(xrq_bf[:, :], xrq_bf[:, :], C["sin"][:, sl], ALU.mult)
                            sV.inc(v.tensor_tensor(qhat[:, dsl], qhat[:, dsl], xrq_bf[:, :], ALU.add))
                        V(b_qro)

                    for tt in range(2):
                        ssn = sS.n

                        def b_zq(t, tt=tt, ssn=ssn):
                            t.wait_ge(sS.h, ssn)
                            t.wait_ge(sV.h, sV.n)
                            sT.inc(t.matmul(psZ[0:1, 0:512], C["ones"][:, :],
                                            exp_q[:, tt * 512:(tt + 1) * 512],
                                            start=True, stop=True))
                        blk.tensor(b_zq)
                        stn = sT.n

                        def b_zqr(v, stn=stn):
                            v.wait_ge(sT.h, stn)
                            sV.inc(v.reciprocal(zqr[:, :], psZ[0:1, 0:512]))
                        V(b_zqr)
                        svn = sV.n

                        def b_zqb(t, svn=svn):
                            t.wait_ge(sV.h, svn)
                            sT.inc(t.matmul(psZ[:, 0:512], C["half"][:, :], zqr[:, :],
                                            start=True, stop=True))
                        blk.tensor(b_zqb)
                        stn = sT.n

                        def b_qt(v, h=h, mc=mc, tt=tt, stn=stn):
                            v.wait_ge(sT.h, stn)
                            v.tensor_copy(zqb[:, :], psZ[:, 0:512])
                            v.tensor_tensor(qt_tmp[:, :],
                                            exp_q[:, tt * 512:(tt + 1) * 512],
                                            zqb[:, :], ALU.mult)
                            for cc_ in range(4):
                                c = tt * 4 + cc_
                                sV.inc(v.tensor_tensor(
                                    qtil[:, mc * T + c * 128:mc * T + (c + 1) * 128],
                                    qt_tmp[:, cc_ * 128:(cc_ + 1) * 128], ea(h, c)[:, :], ALU.mult))
                        V(b_qt)
                        qt_end[0] = sV.n

                svn, stn0 = sV.n, sT.n

                def b_sg(v, h=h, svn=svn, stn0=stn0):
                    v.wait_ge(sV.h, svn)
                    v.wait_ge(sT.h, stn0)
                    v.wait_ge(w_os.h, w_os.n)
                    for c in range(NCH):
                        v.scalar_tensor_tensor(se(h, c)[:, :],
                                               S_in[:, h * 128:(h + 1) * 128],
                                               Dcum[:, h * NCH + c:h * NCH + c + 1],
                                               se(h, c)[:, :], ALU.mult, ALU.add)
                    sV.inc(v.memset(tmp_pc[:, 0:1], 0.0))
                V(b_sg)

                for c in range(NCH):
                    svq = sV.n

                    def b_sc(t, h=h, hp=hp, c=c, svq=svq):
                        t.wait_ge(l_kk[0].h, l_kk[0].n)
                        t.wait_ge(sV.h, svq)
                        t.wait_ge(sS.h, sS.n)
                        j = 2 * c
                        t.matmul(psS[:, 0:256], khat_r[0][:, c * 128:(c + 1) * 128],
                                 qhat_ap(j), start=True, stop=True)
                        t.matmul(psS[:, 256:512], khat_r[0][:, (c + 1) * 128:(c + 2) * 128],
                                 qhat_ap(j), start=True, stop=True)
                        sT.inc(t.matmul(psZ[:, 0:256], khat_r[0][:, (c + 1) * 128:(c + 2) * 128],
                                        qhat_ap(j + 1), start=True, stop=True))
                    blk.tensor(b_sc)
                    stn = sT.n

                    def b_se(s, c=c, stn=stn):
                        s.wait_ge(sT.h, stn)
                        s.activation(eT[0][:, :], psS[:, 0:256], AF.Exp, scale=ISQD, bias=SHIFT)
                        s.activation(eT[1][:, :], psS[:, 256:512], AF.Exp, scale=ISQD, bias=SHIFT)
                        sS.inc(s.activation(eT[2][:, :], psZ[:, 0:256], AF.Exp, scale=ISQD, bias=SHIFT))
                    SC(b_se)
                    ssn = sS.n

                    def b_sm(v, c=c, ssn=ssn):
                        v.wait_ge(sS.h, ssn)
                        mA = C["swevA1"] if c == 0 else C["swevA"]
                        v.tensor_tensor(eT[0][:, :], eT[0][:, :], mA[:, :], ALU.mult)
                        v.tensor_tensor(eT[1][:, :], eT[1][:, :], C["swevB"][:, :], ALU.mult)
                        sV.inc(v.tensor_tensor(eT[2][:, :], eT[2][:, :], C["swodd"][:, :], ALU.mult))
                    V(b_sm)
                    svn = sV.n

                    def b_sz(t, h=h, c=c, svn=svn):
                        t.wait_ge(sV.h, svn)
                        t.matmul(psZ[0:1, 256:512], C["ones"][:, :], eT[0][:, :], start=True, stop=False)
                        t.matmul(psZ[0:1, 256:512], C["ones"][:, :], eT[1][:, :], start=False, stop=True)
                        t.matmul(psS[0:1, 0:256], C["ones"][:, :], eT[2][:, :], start=True, stop=True)
                        t.matmul(ps0[:, 0:256], vt(c)[:, h * 128:(h + 1) * 128], eT[0][:, :],
                                 start=True, stop=False)
                        t.matmul(ps0[:, 0:256], vt(c + 1)[:, h * 128:(h + 1) * 128], eT[1][:, :],
                                 start=False, stop=True)
                        sT.inc(t.matmul(ps0[:, 256:512], vt(c + 1)[:, h * 128:(h + 1) * 128], eT[2][:, :],
                                        start=True, stop=True))
                    blk.tensor(b_sz)
                    stn = sT.n

                    def b_szr(v, stn=stn):
                        v.wait_ge(sT.h, stn)
                        v.reciprocal(zswr[0:1, 0:256], psZ[0:1, 256:512])
                        sV.inc(v.reciprocal(zswr[0:1, 256:512], psS[0:1, 0:256]))
                    V(b_szr)
                    svn = sV.n

                    def b_szb(t, svn=svn):
                        t.wait_ge(sV.h, svn)
                        sT.inc(t.matmul(ps1[:, :], C["half"][:, :], zswr[0:1, :], start=True, stop=True))
                    blk.tensor(b_szb)
                    stn = sT.n

                    def b_sn(v, stn=stn):
                        v.wait_ge(sT.h, stn)
                        v.tensor_copy(zswb[:, :], ps1[:, :])
                        sV.inc(v.tensor_tensor(tswa[:, :], ps0[:, :], zswb[:, :], ALU.mult))
                    V(b_sn)
                    svn = sV.n

                    def b_pt(t, hp=hp, c=c, svn=svn):
                        t.wait_ge(sV.h, svn)
                        sT.inc(t.matmul(ps2[:, :], ktil_r[0][:, c * 128:(c + 1) * 128],
                                        qtil_ap(c), start=True, stop=True))
                    blk.tensor(b_pt)
                    stn = sT.n

                    def b_pm(v, stn=stn):
                        v.wait_ge(sT.h, stn)
                        sV.inc(v.tensor_tensor(pt_bf[:, :], ps2[:, :], C["glam"][:, :], ALU.mult))
                    V(b_pm)
                    svn = sV.n

                    def b_go(t, h=h, c=c, svn=svn):
                        t.wait_ge(sV.h, svn)
                        t.matmul(ps3[:, :], vt(c + 1)[:, h * 128:(h + 1) * 128], pt_bf[:, :],
                                 start=True, stop=False)
                        sT.inc(t.matmul(ps3[:, :], se(h, c)[:, :], qtil_ap(c),
                                        start=False, stop=True))
                    blk.tensor(b_go)
                    stn = sT.n

                    def b_cb(v, c=c, stn=stn):
                        v.wait_ge(sT.h, stn)
                        tsw = tswa[:, :].rearrange("p (j h i) -> p h j i", j=2, h=4)
                        og = ps3[:, :].rearrange("p (j h i) -> p h j i", j=2, h=4)
                        for mc in range(4):
                            sV.inc(v.tensor_tensor(
                                o_buf[:, mc * T + c * 128:mc * T + (c + 1) * 128]
                                .rearrange("p (j i) -> p j i", j=2),
                                tsw[:, mc], og[:, mc], ALU.add))
                    V(b_cb)

                svn = sV.n

                def b_os(s, h=h, svn=svn):
                    s.wait_ge(sV.h, svn)
                    for mc in range(4):
                        w_os.inc(s.dma_start(out=s_o[(4 * h + mc) * 128:(4 * h + mc + 1) * 128, :],
                                             in_=o_buf[:, mc * T:(mc + 1) * T]), 16)
                SC(b_os)
                kk_free[0] = sT.n
                emit_w_chunks(8)

            s1c.__exit__(None, None, None)
            wos_total = w_os.n

        s2 = ExitStack()
        with s2:

            s2b = ExitStack()
            s2b.__enter__()
            trash6 = sb(s2b, "trash6", [128, DM], BF16)
            atok = [sb(s2b, "atok0", [128, DM], BF16), sb(s2b, "atok1", [128, DM], BF16)]
            xrb0 = sb(s2b, "xrb0", [128, DM], F32)
            xrb = [xrb0, xrb0]
            x2b = [sb(s2b, "x2b0", [128, DM], F32), sb(s2b, "x2b1", [128, DM], F32)]
            ln3b = [sb(s2b, "ln3b0", [128, DM], BF16), sb(s2b, "ln3b1", [128, DM], BF16)]
            tat = [Sem(octx, nc, "tat0"), Sem(octx, nc, "tat1")]
            lx2 = [Sem(octx, nc, "lx20"), Sem(octx, nc, "lx21")]
            wx2 = [Sem(octx, nc, "wx2a"), Sem(octx, nc, "wx2b")]
            wl3 = [Sem(octx, nc, "wl3a"), Sem(octx, nc, "wl3b")]
            w_att = [Sem(octx, nc, "w_atta"), Sem(octx, nc, "w_attb")]
            tr3h = [Sem(octx, nc, "tr3h0"), Sem(octx, nc, "tr3h1")]
            wx2_hist, wl3_hist, sv_x2_hist = [], [], []
            q6_state = {}

            def emit_q6_tile(tc):
                p = tc % 2
                svq = sv_x2_hist[tc - 2] if tc >= 2 else 0

                def q6_ld(s, tc=tc, p=p, svq=svq):
                    if tc < 4:
                        s.wait_ge(w_att[0].h, q6_state["half0"])
                    elif tc == 4:
                        s.wait_ge(w_att[1].h, q6_state["half1"])
                    if svq:
                        s.wait_ge(sV.h, svq)
                    tat[p].inc(s.dma_start(out=atok[p][:, :],
                                           in_=s_att[:, tc * 128:(tc + 1) * 128],
                                           transpose=True), 16)
                blk.sync(q6_ld)
                xq = sv_x2_hist[tc - 1] if tc >= 1 else 0

                def q6_ldx(s, tc=tc, p=p, xq=xq):
                    if xq:
                        s.wait_ge(sV.h, xq)
                    lx2[p].inc(s.dma_start(out=xrb[p][:, :], in_=x_seg[tc * 128:(tc + 1) * 128, :]), 16)
                blk.sync(q6_ldx)

                def q6_cv(v, p=p):
                    v.wait_ge(tat[p].h, tat[p].n)
                    sV.inc(v.reduce_sum(stt[p][:, 0:1], atok[p][:, :], axis=mybir.AxisListType.X))
                V(q6_cv)
                svn = sV.n

                def q6_ss(s, p=p, svn=svn):
                    s.wait_ge(sV.h, svn)
                    sS.inc(s.activation(trash6[:, :], atok[p][:, :], AF.Square, accum_out=stt[p][:, 1:2]))
                SC(q6_ss)
                ssn = sS.n

                def q6_mk(v, p=p, ssn=ssn):
                    v.wait_ge(sS.h, ssn)
                    st = stt[p]
                    v.tensor_scalar(st[:, 2:3], st[:, 0:1], 1.0 / DM, None, ALU.mult)
                    v.tensor_scalar(st[:, 3:4], st[:, 1:2], 1.0 / DM, None, ALU.mult)
                    v.tensor_tensor(st[:, 4:5], st[:, 2:3], st[:, 2:3], ALU.mult)
                    sV.inc(v.tensor_tensor(st[:, 3:4], st[:, 3:4], st[:, 4:5], ALU.subtract))
                V(q6_mk)
                svn = sV.n

                def q6_sq(s, p=p, svn=svn):
                    s.wait_ge(sV.h, svn)
                    sS.inc(s.activation(stt[p][:, 4:5], stt[p][:, 3:4], AF.Sqrt, bias=EPS))
                SC(q6_sq)
                ssn = sS.n

                wx2q = wx2_hist[tc - 2] if tc >= 2 else 0

                def q6_x2(v, p=p, ssn=ssn, wx2q=wx2q):
                    v.wait_ge(sS.h, ssn)
                    v.wait_ge(lx2[p].h, lx2[p].n)
                    if wx2q:
                        v.wait_ge(wx2[p].h, wx2q)
                    st = stt[p]
                    v.reciprocal(st[:, 5:6], st[:, 4:5])
                    v.tensor_tensor(st[:, 6:7], st[:, 2:3], st[:, 5:6], ALU.mult)
                    v.tensor_scalar(st[:, 6:7], st[:, 6:7], -1.0, None, ALU.mult)
                    v.tensor_scalar(x2b[p][:, :], atok[p][:, :], st[:, 5:6], st[:, 6:7],
                                    ALU.mult, ALU.add)
                    sV.inc(v.tensor_tensor(x2b[p][:, :], x2b[p][:, :], xrb[p][:, :], ALU.add))
                V(q6_x2)
                sv_x2_hist.append(sV.n)
                svn = sV.n

                def q6_sv3(v, p=p, svn=svn):
                    sV.inc(v.reduce_sum(stt[p][:, 0:1], x2b[p][:, :], axis=mybir.AxisListType.X))
                V(q6_sv3)
                svn2 = sV.n

                def q6_ss3(s, p=p, svn2=svn2):
                    s.wait_ge(sV.h, svn2)
                    sS.inc(s.activation(trash6[:, :], x2b[p][:, :], AF.Square, accum_out=stt[p][:, 1:2]))
                SC(q6_ss3)
                ssn = sS.n

                def q6_mk3(v, p=p, ssn=ssn):
                    v.wait_ge(sS.h, ssn)
                    st = stt[p]
                    v.tensor_scalar(st[:, 2:3], st[:, 0:1], 1.0 / DM, None, ALU.mult)
                    v.tensor_scalar(st[:, 3:4], st[:, 1:2], 1.0 / DM, None, ALU.mult)
                    v.tensor_tensor(st[:, 4:5], st[:, 2:3], st[:, 2:3], ALU.mult)
                    sV.inc(v.tensor_tensor(st[:, 3:4], st[:, 3:4], st[:, 4:5], ALU.subtract))
                V(q6_mk3)
                svn = sV.n

                def q6_sq3(s, p=p, svn=svn):
                    s.wait_ge(sV.h, svn)
                    sS.inc(s.activation(stt[p][:, 4:5], stt[p][:, 3:4], AF.Sqrt, bias=EPS))
                SC(q6_sq3)
                ssn = sS.n

                wl3q = wl3_hist[tc - 2] if tc >= 2 else 0

                def q6_l3(v, p=p, ssn=ssn, wl3q=wl3q):
                    v.wait_ge(sS.h, ssn)
                    if wl3q:
                        v.wait_ge(wl3[p].h, wl3q)
                    st = stt[p]
                    v.reciprocal(st[:, 5:6], st[:, 4:5])
                    v.tensor_tensor(st[:, 6:7], st[:, 2:3], st[:, 5:6], ALU.mult)
                    v.tensor_scalar(st[:, 6:7], st[:, 6:7], -1.0, None, ALU.mult)
                    sV.inc(v.tensor_scalar(ln3b[p][:, :], x2b[p][:, :], st[:, 5:6], st[:, 6:7],
                                           ALU.mult, ALU.add))
                V(q6_l3)
                svn = sV.n

                def q6_w(s, tc=tc, p=p, svn=svn):
                    s.wait_ge(sV.h, svn)
                    wx2[p].inc(s.dma_start(out=s_x2[tc * 128:(tc + 1) * 128, :], in_=x2b[p][:, :]), 16)
                    wl3[p].inc(s.dma_start(out=s_xln3[tc * 128:(tc + 1) * 128, :], in_=ln3b[p][:, :]), 16)
                SC(q6_w)
                wx2_hist.append(wx2[p].n)
                wl3_hist.append(wl3[p].n)

            s2a = ExitStack()
            with s2a:
                o_sb = sb(s2a, "o_sb", [128, 32 * T], BF16)
                att_f = sb(s2a, "att_f", [128, 4 * T], BF16)
                wo_sb = [sb(s2a, "wo0", [128, 8192], BF16), sb(s2a, "wo1", [128, 8192], BF16)]
                g_wo = [Sem(octx, nc, "gwo0"), Sem(octx, nc, "gwo1")]
                sE6 = Sem(octx, nc, "sE6")
                lo_g = [Sem(octx, nc, f"lo{i}") for i in range(4)]

                def p6_lo(s):
                    s.wait_ge(w_os.h, wos_total)
                    for kc in range(32):
                        lo_g[kc // 8].inc(s.dma_start(out=o_sb[:, kc * T:(kc + 1) * T],
                                                      in_=s_o[kc * 128:(kc + 1) * 128, :]), 16)
                blk.sync(p6_lo)

                wo_free = {}
                g6 = [0]
                ev6_hist = []
                for tt in range(2):
                    for mb in range(16):
                        p = (tt * 16 + mb) % 2
                        stq = wo_free.get(p, 0)

                        def p6_w(s, tt=tt, mb=mb, p=p, stq=stq):
                            if tt == 0 and mb < 2:
                                s.wait_ge(w_os.h, wos_total)
                                s.wait_ge(ag_w["wo"].h, 1)
                            else:
                                s.wait_ge(sT.h, stq)
                            for kcg in range(2):
                                g_wo[p].inc(s.dma_start(
                                    out=wo_sb[p][:, kcg * 4096:(kcg + 1) * 4096]
                                    .rearrange("p (kc j) -> p kc j", j=256),
                                    in_=s_wob[kcg * 2048:(kcg + 1) * 2048, mb * 256:(mb + 1) * 256]
                                    .rearrange("(kc p) j -> p kc j", p=128)), 16)
                        blk.sync(p6_w)

                        for mc2 in range(2):
                            gi = g6[0]
                            myps = PSROT[gi % 4]
                            g6[0] += 1
                            svq = ev6_hist[gi - 4] if gi >= 4 else 0

                            def p6_mm(t, tt=tt, mb=mb, p=p, mc2=mc2, myps=myps, svq=svq):
                                t.wait_ge(g_wo[p].h, g_wo[p].n)
                                for g in lo_g:
                                    t.wait_ge(g.h, g.n)
                                if svq:
                                    t.wait_ge(sE6.h, svq)
                                for kc in range(32):
                                    ins = t.matmul(
                                        myps[:, :],
                                        wo_sb[p][:, kc * 256 + mc2 * 128:kc * 256 + (mc2 + 1) * 128],
                                        o_sb[:, kc * T + tt * 512:kc * T + (tt + 1) * 512],
                                        start=(kc == 0), stop=(kc == 31))
                                    if kc == 31:
                                        sT.inc(ins)
                            blk.tensor(p6_mm)
                            stn = sT.n

                            def p6_ev(v, tt=tt, mb=mb, mc2=mc2, myps=myps, stn=stn):
                                v.wait_ge(sT.h, stn)
                                v.wait_ge(w_att[tt].h, w_att[tt].n)
                                sl = (mb * 2 + mc2) % 4
                                sE6.inc(v.tensor_copy(
                                    att_f[:, sl * T + tt * 512:sl * T + (tt + 1) * 512],
                                    myps[:, :]))
                            V(p6_ev)
                            ev6_hist.append(sE6.n)
                            svn = sE6.n

                            def p6_wr(s, tt=tt, mb=mb, mc2=mc2, svn=svn):
                                s.wait_ge(sE6.h, svn)
                                ch = mb * 2 + mc2
                                sl = ch % 4
                                w_att[tt].inc(s.dma_start(
                                    out=s_att[ch * 128:(ch + 1) * 128, tt * 512:(tt + 1) * 512],
                                    in_=att_f[:, sl * T + tt * 512:sl * T + (tt + 1) * 512]), 16)
                            SC(p6_wr)
                        wo_free[p] = sT.n
                        if tt == 1 and mb in (3, 7, 11, 15):
                            emit_q6_tile((mb - 3) // 4)
                    if tt == 0:
                        q6_state["half0"] = w_att[0].n
                q6_state["half1"] = w_att[1].n
                watt_total = (w_att[0].n, w_att[1].n)

            for tc in (4, 5, 6, 7):
                emit_q6_tile(tc)
            wx2_total = (wx2[0].n, wx2[1].n)
            wl3_total = (wl3[0].n, wl3[1].n)
            s2b.__exit__(None, None, None)

            s2h = ExitStack()
            s2h.__enter__()
            h_feat = sb(s2h, "h_feat", [128, 32 * T], BF16)

            emit_w_chunks(1000)
            s2c = ExitStack()
            with s2c:
                x1s = sb(s2c, "x1s", [128, 4 * T], F32)
                w12t = [sb(s2c, "w12t0", [128, 16384], BF16), sb(s2c, "w12t1", [128, 16384], BF16)]
                xln3T = sb(s2c, "xln3T", [128, 32 * T], BF16)
                twt = [Sem(octx, nc, "twt0"), Sem(octx, nc, "twt1")]

                for hf in range(2):
                    def p7_tr(s, hf=hf):
                        for _i in range(2):
                            s.wait_ge(wl3[_i].h, (wl3_hist[3] if hf == 0 else wl3_total[_i])
                                      if _i == 1 else (wl3_hist[2] if hf == 0 else wl3_total[_i]))
                        for kc in range(32):
                            tr3h[hf].inc(s.dma_start(
                                out=xln3T[:, kc * T + hf * 512:kc * T + hf * 512 + 512],
                                in_=s_xln3[hf * 512:(hf + 1) * 512, kc * 128:(kc + 1) * 128],
                                transpose=True), 16)
                    blk.sync(p7_tr)
                tr3h_tot = (tr3h[0].n, tr3h[1].n)

                w12_free = [0, 0]
                cons7_hist = []
                si_map = {}
                g7 = [0]
                for pr in range(8):
                    for half in range(2):
                        mrow = pr * 512 + half * 4096
                        p = (2 * pr + half) % 2
                        stq = w12_free[p]

                        def p7_w(s, mrow=mrow, p=p, stq=stq, pr=pr, half=half):
                            if pr == 0 and half < 2:
                                s.wait_ge(ag_w["w12"].h, 1)
                                for _i in range(2):
                                    s.wait_ge(wl3[_i].h, wl3_total[_i])
                                    s.wait_ge(wx2[_i].h, wx2_total[_i])
                            if 2 * pr + half >= 2:
                                s.wait_ge(sT.h, stq)
                            r, j0 = mrow // 1024, mrow % 1024
                            for qq in range(4):
                                twt[p].inc(s.dma_start(
                                    out=w12t[p][:, qq * 4096:(qq + 1) * 4096]
                                    .rearrange("p (kc j) -> p kc j", j=512),
                                    in_=s_w12T[r * DM + qq * 1024:r * DM + (qq + 1) * 1024,
                                               j0:j0 + 512]
                                    .rearrange("(kc p) j -> p kc j", p=128)), 16)
                        blk.sync(p7_w)

                        for mc2 in range(4):
                            for tt in range(2):
                                myps = PSROT[(2 * mc2 + tt) % 4]
                                gi = g7[0]
                                g7[0] += 1
                                cons = cons7_hist[gi - 4] if gi >= 4 else None

                                def p7_mm(t, p=p, mc2=mc2, tt=tt, myps=myps, cons=cons):
                                    t.wait_ge(twt[p].h, twt[p].n)
                                    t.wait_ge(tr3h[tt].h, tr3h_tot[tt])
                                    if cons is not None:
                                        eng, cnt = cons
                                        t.wait_ge(sS.h if eng == "S" else sV.h, cnt)
                                    for kc in range(32):
                                        ins = t.matmul(
                                            myps[:, :],
                                            w12t[p][:, kc * 512 + mc2 * 128:kc * 512 + (mc2 + 1) * 128],
                                            xln3T[:, kc * T + tt * 512:kc * T + (tt + 1) * 512],
                                            start=(kc == 0), stop=(kc == 31))
                                        if kc == 31:
                                            sT.inc(ins)
                                blk.tensor(p7_mm)
                                stn = sT.n
                                mcol = (mrow // 128) + mc2

                                if half == 0:
                                    def p7_si(s, mcol=mcol, mc2=mc2, tt=tt, myps=myps, stn=stn):
                                        s.wait_ge(sT.h, stn)
                                        sS.inc(s.activation(
                                            x1s[:, mc2 * T + tt * 512:mc2 * T + (tt + 1) * 512],
                                            myps[:, :], AF.Silu, bias=b12sb[:, mcol:mcol + 1]))
                                    SC(p7_si)
                                    cons7_hist.append(("S", sS.n))
                                    si_map[(pr, mc2, tt)] = sS.n
                                else:
                                    siq = si_map[(pr, mc2, tt)]

                                    def p7_h(v, mcol=mcol, pr=pr, mc2=mc2, tt=tt, myps=myps,
                                             stn=stn, siq=siq):
                                        v.wait_ge(sT.h, stn)
                                        v.wait_ge(sS.h, siq)
                                        ch = pr * 4 + mc2
                                        sV.inc(v.scalar_tensor_tensor(
                                            h_feat[:, ch * T + tt * 512:ch * T + (tt + 1) * 512],
                                            myps[:, :], b12sb[:, 32 + mcol - 32 if half else mcol:
                                                             32 + mcol - 32 + 1 if half else mcol + 1],
                                            x1s[:, mc2 * T + tt * 512:mc2 * T + (tt + 1) * 512],
                                            ALU.add, ALU.mult))
                                    V(p7_h)
                                    cons7_hist.append(("V", sV.n))
                        w12_free[p] = sT.n

            s2e = ExitStack()
            s2e.__enter__()
            trash9 = sb(s2e, "trash9", [128, DM], BF16)
            ftok = [sb(s2e, "ftok0", [128, DM], BF16), sb(s2e, "ftok1", [128, DM], BF16)]
            x2r0 = sb(s2e, "x2r0", [128, DM], F32)
            x2r = [x2r0, x2r0]
            outb = [sb(s2e, "outb0", [128, DM], F32), sb(s2e, "outb1", [128, DM], F32)]
            tft = [Sem(octx, nc, "tft0"), Sem(octx, nc, "tft1")]
            lx3 = [Sem(octx, nc, "lx30"), Sem(octx, nc, "lx31")]
            w_out = [Sem(octx, nc, "w_outa"), Sem(octx, nc, "w_outb")]
            w_ffn = [Sem(octx, nc, "w_ffna"), Sem(octx, nc, "w_ffnb")]
            wout_hist, sv_p9_hist = [], []
            p9_state = {}

            def emit_p9_tile(tc):
                p = tc % 2
                svq = sv_p9_hist[tc - 2] if tc >= 2 else 0

                def p9_ld(s, tc=tc, p=p, svq=svq):
                    if tc == 0:
                        s.wait_ge(w_ffn[0].h, p9_state["half0"])
                        s.wait_ge(wx2[0].h, wx2_total[0])
                        s.wait_ge(wx2[1].h, wx2_total[1])
                    elif tc == 4:
                        s.wait_ge(w_ffn[1].h, p9_state["half1"])
                    if svq:
                        s.wait_ge(sV.h, svq)
                    tft[p].inc(s.dma_start(out=ftok[p][:, :],
                                           in_=s_ffn[:, tc * 128:(tc + 1) * 128],
                                           transpose=True), 16)
                blk.sync(p9_ld)
                xq = sv_p9_hist[tc - 1] if tc >= 1 else 0

                def p9_ldx(s, tc=tc, p=p, xq=xq):
                    if xq:
                        s.wait_ge(sV.h, xq)
                    lx3[p].inc(s.dma_start(out=x2r[p][:, :], in_=s_x2[tc * 128:(tc + 1) * 128, :]), 16)
                blk.sync(p9_ldx)

                def p9_cv(v, p=p):
                    v.wait_ge(tft[p].h, tft[p].n)
                    sV.inc(v.reduce_sum(stt[p][:, 0:1], ftok[p][:, :], axis=mybir.AxisListType.X))
                V(p9_cv)
                svn = sV.n

                def p9_ss(s, p=p, svn=svn):
                    s.wait_ge(sV.h, svn)
                    sS.inc(s.activation(trash9[:, :], ftok[p][:, :], AF.Square, accum_out=stt[p][:, 1:2]))
                SC(p9_ss)
                ssn = sS.n

                def p9_mk(v, p=p, ssn=ssn):
                    v.wait_ge(sS.h, ssn)
                    st = stt[p]
                    v.tensor_scalar(st[:, 2:3], st[:, 0:1], 1.0 / DM, None, ALU.mult)
                    v.tensor_scalar(st[:, 3:4], st[:, 1:2], 1.0 / DM, None, ALU.mult)
                    v.tensor_tensor(st[:, 4:5], st[:, 2:3], st[:, 2:3], ALU.mult)
                    sV.inc(v.tensor_tensor(st[:, 3:4], st[:, 3:4], st[:, 4:5], ALU.subtract))
                V(p9_mk)
                svn = sV.n

                def p9_sq(s, p=p, svn=svn):
                    s.wait_ge(sV.h, svn)
                    sS.inc(s.activation(stt[p][:, 4:5], stt[p][:, 3:4], AF.Sqrt, bias=EPS))
                SC(p9_sq)
                ssn = sS.n

                woq = wout_hist[tc - 2] if tc >= 2 else 0

                def p9_f(v, p=p, ssn=ssn, woq=woq):
                    v.wait_ge(sS.h, ssn)
                    v.wait_ge(lx3[p].h, lx3[p].n)
                    if woq:
                        v.wait_ge(w_out[p].h, woq)
                    st = stt[p]
                    v.reciprocal(st[:, 5:6], st[:, 4:5])
                    v.tensor_tensor(st[:, 6:7], st[:, 2:3], st[:, 5:6], ALU.mult)
                    v.tensor_scalar(st[:, 6:7], st[:, 6:7], -1.0, None, ALU.mult)
                    v.tensor_scalar(outb[p][:, :], ftok[p][:, :], st[:, 5:6], st[:, 6:7],
                                    ALU.mult, ALU.add)
                    sV.inc(v.tensor_tensor(outb[p][:, :], outb[p][:, :], x2r[p][:, :], ALU.add))
                V(p9_f)
                sv_p9_hist.append(sV.n)
                svn = sV.n

                def p9_w(s, tc=tc, p=p, svn=svn):
                    s.wait_ge(sV.h, svn)
                    w_out[p].inc(s.dma_start(out=out[tc * 128:(tc + 1) * 128, :], in_=outb[p][:, :]), 16)
                SC(p9_w)
                wout_hist.append(w_out[p].n)

            s2d = ExitStack()
            with s2d:
                w3t = [sb(s2d, "w3t0", [128, 16384], BF16), sb(s2d, "w3t1", [128, 16384], BF16)]
                ffn_f = sb(s2d, "ffn_f", [128, 4 * T], BF16)
                tw3 = [Sem(octx, nc, "tw30"), Sem(octx, nc, "tw31")]
                sE8 = Sem(octx, nc, "sE8")

                w3_free = {}
                g8 = [0]
                ev8_hist = []
                stq8, svq8 = sT.n, sV.n
                for tt in range(2):
                    for nb in range(8):
                        p = (tt * 8 + nb) % 2
                        stq = w3_free.get(p, 0)

                        def p8_w(s, tt=tt, nb=nb, p=p, stq=stq):
                            if tt == 0 and nb < 2:
                                s.wait_ge(sT.h, stq8)
                                s.wait_ge(sV.h, svq8)
                                s.wait_ge(ag_w["w3"].h, 1)
                            else:
                                s.wait_ge(sT.h, stq)
                            for qq in range(4):
                                tw3[p].inc(s.dma_start(
                                    out=w3t[p][:, qq * 4096:(qq + 1) * 4096]
                                    .rearrange("p (mc j) -> p mc j", j=512),
                                    in_=s_w3T[nb * DM + qq * 1024:nb * DM + (qq + 1) * 1024, :]
                                    .rearrange("(mc p) j -> p mc j", p=128)), 16)
                        blk.sync(p8_w)

                        for nc2 in range(4):
                            gi = g8[0]
                            myps = PSROT[gi % 4]
                            g8[0] += 1
                            svq = ev8_hist[gi - 4] if gi >= 4 else 0

                            def p8_mm(t, tt=tt, nb=nb, p=p, nc2=nc2, myps=myps, svq=svq):
                                t.wait_ge(tw3[p].h, tw3[p].n)
                                if svq:
                                    t.wait_ge(sE8.h, svq)
                                for mc in range(32):
                                    ins = t.matmul(
                                        myps[:, :],
                                        w3t[p][:, mc * 512 + nc2 * 128:mc * 512 + (nc2 + 1) * 128],
                                        h_feat[:, mc * T + tt * 512:mc * T + (tt + 1) * 512],
                                        start=(mc == 0), stop=(mc == 31))
                                    if mc == 31:
                                        sT.inc(ins)
                            blk.tensor(p8_mm)
                            stn = sT.n
                            ncol = nb * 4 + nc2

                            def p8_ev(v, tt=tt, ncol=ncol, nc2=nc2, myps=myps, stn=stn):
                                v.wait_ge(sT.h, stn)
                                v.wait_ge(w_ffn[tt].h, w_ffn[tt].n)
                                sE8.inc(v.tensor_scalar(
                                    ffn_f[:, nc2 * T + tt * 512:nc2 * T + (tt + 1) * 512],
                                    myps[:, :], b3sb[:, ncol:ncol + 1], None, ALU.add))
                            V(p8_ev)
                            ev8_hist.append(sE8.n)
                            svn = sE8.n

                            def p8_wr(s, tt=tt, nb=nb, nc2=nc2, svn=svn):
                                s.wait_ge(sE8.h, svn)
                                ncol = nb * 4 + nc2
                                w_ffn[tt].inc(s.dma_start(
                                    out=s_ffn[ncol * 128:(ncol + 1) * 128, tt * 512:(tt + 1) * 512],
                                    in_=ffn_f[:, nc2 * T + tt * 512:nc2 * T + (tt + 1) * 512]), 16)
                            SC(p8_wr)
                        w3_free[p] = sT.n
                        if tt == 1 and nb in (1, 3, 5, 7):
                            emit_p9_tile((nb - 1) // 2)
                    if tt == 0:
                        p9_state["half0"] = w_ffn[0].n
                p9_state["half1"] = w_ffn[1].n

            for tc in (4, 5, 6, 7):
                emit_p9_tile(tc)
            wout_total = (w_out[0].n, w_out[1].n)

            def p9_fin(s):
                s.wait_ge(w_out[0].h, wout_total[0])
                s.wait_ge(w_out[1].h, wout_total[1])
            blk.sync(p9_fin)
            s2e.__exit__(None, None, None)

            s2h.__exit__(None, None, None)

    return nc


_CACHE = {}


def _consts_shared():
    if "c" in _CACHE:
        return _CACHE["c"]
    r = np.arange(128)
    i64 = np.arange(64)
    c = {}
    c["ident"] = np.eye(128, dtype=BFNP)
    rot = np.zeros((128, 128), np.float32)
    for p in range(128):
        rot[(p + 64) % 128, p] = 1.0
    c["rot"] = rot.astype(BFNP)
    s_ = r[:, None]
    t_ = r[None, :]
    c["lt"] = (((s_ <= t_).astype(np.float32)) / 16.0).astype(BFNP)
    c["uts"] = (((s_ > t_).astype(np.float32)) / 16.0).astype(BFNP)
    c["ones"] = np.ones((128, 1), BFNP)
    c["half"] = np.full((1, 128), 0.5, BFNP)
    c["onerow"] = np.ones((1, 128), BFNP)
    j2 = np.arange(512) // 256
    ii = np.arange(512) % 64
    tau = j2 * 64 + ii
    c["glam"] = ((s_ <= tau[None, :]).astype(np.float32)).astype(BFNP)
    iw = np.tile(np.arange(64), 4)[None, :]
    rr = r[:, None]
    c["swodd"] = (((iw <= rr) & (rr <= iw + 64)).astype(np.float32)).astype(BFNP)
    c["swevA"] = ((rr >= 64 + iw).astype(np.float32)).astype(BFNP)
    c["swevB"] = ((rr <= iw).astype(np.float32)).astype(BFNP)
    _CACHE["c"] = c
    return c


def _core_tables(core):
    key = ("t", core)
    if key in _CACHE:
        return _CACHE[key]
    s0 = (core % 4) * T
    pos = np.arange(s0 - 128, s0 + T).astype(np.float64)
    inv = 10000.0 ** (-np.arange(64, dtype=np.float64) / 64.0)
    fr = np.outer(inv, pos)
    cos = np.concatenate([np.cos(fr), np.cos(fr)], axis=0).astype(BFNP)
    sin = np.concatenate([-np.sin(fr), np.sin(fr)], axis=0).astype(BFNP)
    eoh = np.zeros((128, 8), np.float32)
    eoh[:, core] = 1.0
    cshared = _consts_shared()
    evA1 = np.zeros((128, 256), BFNP) if s0 == 0 else cshared["swevA"]
    _CACHE[key] = (cos, sin, eoh, evA1)
    return _CACHE[key]


def kernel(L0_attn_hidden_states, L0_attn_Wq, L0_attn_Wk, L0_attn_Wv, L0_attn_Wo,
           L0_ffn_w12, L0_ffn_b12, L0_ffn_w3, L0_ffn_b3):
    x = np.asarray(L0_attn_hidden_states)
    Wq = np.ascontiguousarray(L0_attn_Wq)
    Wk = np.ascontiguousarray(L0_attn_Wk)
    Wv = np.ascontiguousarray(L0_attn_Wv)
    Wo = np.ascontiguousarray(L0_attn_Wo)
    w12 = np.ascontiguousarray(L0_ffn_w12)
    w3 = np.ascontiguousarray(L0_ffn_w3)
    b12s = np.ascontiguousarray(np.asarray(L0_ffn_b12).reshape(64, 128).T)
    b3s = np.ascontiguousarray(np.asarray(L0_ffn_b3).reshape(32, 128).T)

    if "nc" not in _CACHE:
        _CACHE["nc"] = build_program()
    nc = _CACHE["nc"]

    cshared = _consts_shared()
    in_maps = []
    for core in range(NCORES):
        b, s0 = core // 4, (core % 4) * T
        cos, sin, eoh, evA1 = _core_tables(core)
        if s0 == 0:
            halo = np.zeros((128, DM), np.float32)
        else:
            halo = np.ascontiguousarray(x[b, s0 - 128:s0])
        m = {
            "x_seg": np.ascontiguousarray(x[b, s0:s0 + T]),
            "x_halo": halo,
            "Wq_sh": np.ascontiguousarray(Wq[core * 512:(core + 1) * 512]),
            "Wk": Wk, "Wv": Wv,
            "Wo_sh": np.ascontiguousarray(Wo[core * 512:(core + 1) * 512]),
            "w12_sh": np.ascontiguousarray(w12[core * 1024:(core + 1) * 1024]),
            "w3_sh": np.ascontiguousarray(w3[core * 512:(core + 1) * 512]),
            "b12s": b12s, "b3s": b3s,
            "ci_ident": cshared["ident"], "ci_rot": cshared["rot"],
            "ci_lt": cshared["lt"], "ci_uts": cshared["uts"],
            "ci_ones": cshared["ones"], "ci_half": cshared["half"],
            "ci_onerow": cshared["onerow"], "ci_glam": cshared["glam"],
            "ci_swodd": cshared["swodd"], "ci_swevA": cshared["swevA"],
            "ci_swevB": cshared["swevB"], "ci_swevA1": evA1,
            "ci_cos": cos, "ci_sin": sin, "ci_eoh": eoh,
            "cvals_f": np.tile(np.array([[EPS, SHIFT, ISQD, -1.0, 1.0 / DM, 0.5, 1.0, 0.0]],
                                        np.float32), (128, 1)),
            "cvals_b": np.tile(np.array([[SHIFT, -1.0, 1.0, 0.0]], np.float32).astype(BFNP),
                               (128, 1)),
        }
        in_maps.append(m)

    res = run_bass_kernel_spmd(nc, in_maps, core_ids=list(range(NCORES)))
    outp = np.empty((2, 4096, DM), np.float32)
    for core in range(NCORES):
        b, s0 = core // 4, (core % 4) * T
        outp[b, s0:s0 + T] = res.results[core]["out"]
    return outp


# version 23 (final)
# speedup vs baseline: 1.0785x; 1.0785x over previous
import numpy as np
import ml_dtypes
from contextlib import ExitStack

import concourse.bass as bass
import concourse.mybir as mybir
from concourse.bass_utils import run_bass_kernel_spmd

F32 = mybir.dt.float32
BF16 = mybir.dt.bfloat16
AF = mybir.ActivationFunctionType
ALU = mybir.AluOpType
BFNP = ml_dtypes.bfloat16

NCORES = 8
T = 1024
TH = 1152
KVH = 8
DM = 4096
FF = 8192
NCH = 8
EPS = 1e-5
SHIFT = -8.0
ISQD = float(1.0 / np.sqrt(128.0))


class Sem:
    def __init__(self, ctx, nc, name):
        self.h = ctx.enter_context(nc.semaphore(name))
        self.n = 0

    def inc(self, instr, k=1):
        instr.then_inc(self.h, k)
        self.n += k
        return instr


_NO_TRACK = {"drain", "wait_ge", "dma_start", "dma_start_transpose", "wait",
             "then_inc", "collective_compute"}
_DW_STATE = {}


def _ap_iv(a):
    try:
        nm = a.name
        off = a.offset
        sh = a.shape
    except Exception:
        return None
    if len(sh) <= 2:
        span = sh[1] if len(sh) == 2 else 1
        return (nm, off, off + span)
    return (nm, 0, 1 << 60)


def _is_ap(a):
    return hasattr(a, "offset") and hasattr(a, "name") and hasattr(a, "shape") \
        and not isinstance(a, (int, float, str))


class _DW:

    def __init__(self, eng, key):
        self._e = eng
        self._k = key
        if key not in _DW_STATE:
            _DW_STATE[key] = ([], [])

    def __getattr__(self, name):
        a = getattr(self._e, name)
        if name in _NO_TRACK or name.startswith("_") or not callable(a):
            return a

        def wrapped(*args, **kw):
            wr, rd = [], []
            for i, arg in enumerate(args):
                if _is_ap(arg):
                    iv = _ap_iv(arg)
                    if iv is not None:
                        (wr if i == 0 else rd).append(iv)
            for kname, arg in kw.items():
                if _is_ap(arg):
                    iv = _ap_iv(arg)
                    if iv is not None:
                        (wr if kname in ("accum_out", "out") else rd).append(iv)
            if not wr and not rd:
                return a(*args, **kw)
            dwr, drd = _DW_STATE[self._k]

            def _hit(ivs, vs):
                for nm, lo, hi in ivs:
                    for nm2, lo2, hi2 in vs:
                        if nm == nm2 and lo < hi2 and lo2 < hi:
                            return True
                return False
            if _hit(wr, dwr) or _hit(rd, dwr) or _hit(wr, drd):
                self._e.drain()
                dwr.clear()
                drd.clear()
            r = a(*args, **kw)
            dwr.extend(wr)
            drd.extend(rd)
            return r
        return wrapped


def build_program():
    _DW_STATE.clear()
    nc = bass.Bass("TRN2", target_bir_lowering=False, dynamic_dma_scratch_size=8192)

    x_seg = nc.dram_tensor("x_seg", [T, DM], F32, kind="ExternalInput")
    x_halo = nc.dram_tensor("x_halo", [128, DM], F32, kind="ExternalInput")
    Wq_sh = nc.dram_tensor("Wq_sh", [512, DM], F32, kind="ExternalInput")
    Wk = nc.dram_tensor("Wk", [DM, 1024], F32, kind="ExternalInput")
    Wv = nc.dram_tensor("Wv", [DM, 1024], F32, kind="ExternalInput")
    Wo_sh = nc.dram_tensor("Wo_sh", [512, DM], F32, kind="ExternalInput")
    w12_sh = nc.dram_tensor("w12_sh", [1024, DM], F32, kind="ExternalInput")
    w3_sh = nc.dram_tensor("w3_sh", [512, DM], F32, kind="ExternalInput")
    b12s = nc.dram_tensor("b12s", [128, 64], F32, kind="ExternalInput")
    b3s = nc.dram_tensor("b3s", [128, 32], F32, kind="ExternalInput")
    cvals_f = nc.dram_tensor("cvals_f", [128, 8], F32, kind="ExternalInput")
    cvals_b = nc.dram_tensor("cvals_b", [128, 4], BF16, kind="ExternalInput")
    ci = {}
    for nm, sh in [("ident", [128, 128]), ("rot", [128, 128]), ("lt", [128, 128]),
                   ("uts", [128, 128]), ("ones", [128, 1]), ("half", [1, 128]),
                   ("onerow", [1, 128]), ("glam", [128, 512]), ("swodd", [128, 256]),
                   ("swevA", [128, 256]), ("swevB", [128, 256]), ("swevA1", [128, 256]),
                   ("cos", [128, TH]), ("sin", [128, TH])]:
        ci[nm] = nc.dram_tensor("ci_" + nm, sh, BF16, kind="ExternalInput")
    ci["eoh"] = nc.dram_tensor("ci_eoh", [128, 8], F32, kind="ExternalInput")

    out = nc.dram_tensor("out", [T, DM], F32, kind="ExternalOutput")

    s_xln1 = nc.dram_tensor("s_xln1", [TH, DM], BF16, kind="Internal")
    s_wkb = nc.dram_tensor("s_wkb", [DM, 1024], BF16, kind="Internal")
    s_wvb = nc.dram_tensor("s_wvb", [DM, 1024], BF16, kind="Internal")
    sh_wq = nc.dram_tensor("sh_wq", [512, DM], BF16, kind="Internal")
    sh_wo = nc.dram_tensor("sh_wo", [512, DM], BF16, kind="Internal")
    sh_w12b = nc.dram_tensor("sh_w12b", [1024, DM], BF16, kind="Internal")
    sh_w3b = nc.dram_tensor("sh_w3b", [512, DM], BF16, kind="Internal")
    sh_w12T = nc.dram_tensor("sh_w12T", [DM, 1024], BF16, kind="Internal")
    sh_w3T = nc.dram_tensor("sh_w3T", [DM, 512], BF16, kind="Internal")
    s_wqb = nc.dram_tensor("s_wqb", [DM, DM], BF16, kind="Internal",
                           addr_space="Shared")
    s_wob = nc.dram_tensor("s_wob", [DM, DM], BF16, kind="Internal",
                           addr_space="Shared")
    s_w12T = nc.dram_tensor("s_w12T", [NCORES * DM, 1024], BF16, kind="Internal",
                            addr_space="Shared")
    s_w3T = nc.dram_tensor("s_w3T", [NCORES * DM, 512], BF16, kind="Internal",
                           addr_space="Shared")
    s_khat = nc.dram_tensor("s_khat", [KVH * 128, TH], BF16, kind="Internal")
    s_ktil = nc.dram_tensor("s_ktil", [KVH * 128, T], BF16, kind="Internal")
    s_o = nc.dram_tensor("s_o", [DM, T], BF16, kind="Internal")
    s_att = nc.dram_tensor("s_att", [DM, T], BF16, kind="Internal")
    s_x2 = nc.dram_tensor("s_x2", [T, DM], F32, kind="Internal")
    s_xln3 = nc.dram_tensor("s_xln3", [T, DM], BF16, kind="Internal")
    s_ffn = nc.dram_tensor("s_ffn", [DM, T], BF16, kind="Internal")
    ag_in = nc.dram_tensor("ag_in", [128, KVH * 129], BF16, kind="Internal")
    ag_out = nc.dram_tensor("ag_out", [NCORES * 128, KVH * 129], BF16, kind="Internal",
                            addr_space="Shared")

    octx = ExitStack()
    with octx:
        octx.enter_context(nc.allow_low_precision(
            "bf16 matmul operands / intermediates, fp32 accumulation (per problem tolerance)"))
        blk = octx.enter_context(nc.Block())

        def V(f):
            blk.vector(lambda v: f(_DW(v, "v")))

        def SC(f):
            blk.scalar(lambda s: f(_DW(s, "s")))

        sT = Sem(octx, nc, "sT")
        sV = Sem(octx, nc, "sV")
        sS = Sem(octx, nc, "sS")

        def sb(ctx, name, shape, dt):
            return ctx.enter_context(nc.sbuf_tensor(name, shape, dt))

        ps0 = octx.enter_context(nc.psum_tensor("ps0", [128, 512], F32))
        ps1 = octx.enter_context(nc.psum_tensor("ps1", [128, 512], F32))
        ps2 = octx.enter_context(nc.psum_tensor("ps2", [128, 512], F32))
        ps3 = octx.enter_context(nc.psum_tensor("ps3", [128, 512], F32))
        psS = octx.enter_context(nc.psum_tensor("psS", [128, 512], F32))
        psZ = octx.enter_context(nc.psum_tensor("psZ", [128, 512], F32))
        psT = octx.enter_context(nc.psum_tensor("psT", [128, 128], BF16))
        psT2 = octx.enter_context(nc.psum_tensor("psT2", [128, 128], BF16))
        PSROT = [ps0, ps1, ps2, ps3]

        C = {nm: sb(octx, "k_" + nm, t.shape, t.dtype) for nm, t in ci.items()
             if nm not in ("cos", "sin")}
        cvf = sb(octx, "k_cvf", [128, 8], F32)
        cvb = sb(octx, "k_cvb", [128, 4], BF16)
        for dt_, tile, vals in [
            (F32, cvf, [EPS, SHIFT, ISQD, -1.0, 1.0 / DM, 0.5, 1.0, 0.0]),
            (BF16, cvb, [SHIFT, -1.0, 1.0, 0.0]),
        ]:
            for col, val in enumerate(vals):
                try:
                    nc.const_aps.register(dt_, float(val), tile[:, col:col + 1])
                except Exception:
                    pass
        b12sb = sb(octx, "k_b12", [128, 64], F32)
        b3sb = sb(octx, "k_b3", [128, 32], F32)
        stt = [sb(octx, "k_st0", [128, 8], F32), sb(octx, "k_st1", [128, 8], F32)]
        s_cp = {nm: Sem(octx, nc, "scp_" + nm) for nm in ["wk", "wv", "wq", "wo", "w12", "w3"]}
        ag_w = {nm: Sem(octx, nc, "agw_" + nm) for nm in ["wq", "wo", "w12", "w3"]}
        wT_buf = [sb(octx, "wTb0", [128, 512], BF16), sb(octx, "wTb1", [128, 512], BF16)]
        s_wTt = [Sem(octx, nc, "wTt0"), Sem(octx, nc, "wTt1")]
        s_wTw = [Sem(octx, nc, "wTw0"), Sem(octx, nc, "wTw1")]

        sem_c0 = Sem(octx, nc, "c0")

        def ph0_consts(s):
            for nm in C:
                pass
            for nm in [n for n in ci if n not in ("cos", "sin")]:
                sem_c0.inc(s.dma_start(out=C[nm][:, :], in_=ci[nm][:, :]), 16)
            sem_c0.inc(s.dma_start(out=b12sb[:, :], in_=b12s[:, :]), 16)
            sem_c0.inc(s.dma_start(out=b3sb[:, :], in_=b3s[:, :]), 16)
            sem_c0.inc(s.dma_start(out=cvf[:, :], in_=cvals_f[:, :]), 16)
            sem_c0.inc(s.dma_start(out=cvb[:, :], in_=cvals_b[:, :]), 16)
            s.wait_ge(sem_c0.h, sem_c0.n)
        blk.sync(ph0_consts)
        for _eng in ("tensor", "vector", "scalar", "gpsimd"):
            getattr(blk, _eng)(lambda e: e.wait_ge(sem_c0.h, sem_c0.n))

        def w_copies(g):
            for nm, srcT, dstT in [("wk", Wk, s_wkb), ("wv", Wv, s_wvb)]:
                s_cp[nm].inc(g.dma_start(out=dstT[:, :], in_=srcT[:, :]), 16)
        blk.gpsimd(w_copies)

        def w_ags_pre(g):
            g.wait_ge(s_cp["wq"].h, 16)
            ag_w["wq"].inc(g.collective_compute(
                "AllGather", ALU.bypass, replica_groups=[list(range(NCORES))],
                ins=[sh_wq[:, :]], outs=[s_wqb[:, :]]), 1)
            g.wait_ge(s_cp["wo"].h, 16)
            ag_w["wo"].inc(g.collective_compute(
                "AllGather", ALU.bypass, replica_groups=[list(range(NCORES))],
                ins=[sh_wo[:, :]], outs=[s_wob[:, :]]), 1)

        _wT_jobs = []
        for k0 in range(0, DM, 128):
            for m0 in range(0, 1024, 512):
                _wT_jobs.append((sh_w12b, sh_w12T, m0, k0))
        for k0 in range(0, DM, 128):
            _wT_jobs.append((sh_w3b, sh_w3T, 0, k0))
        _wT_state = {"i": 0}

        def emit_w_chunks(n):
            i0 = _wT_state["i"]
            if i0 >= len(_wT_jobs):
                return
            i1 = min(i0 + n, len(_wT_jobs))
            _wT_state["i"] = i1

            def go(s, i0=i0, i1=i1):
                for i in range(i0, i1):
                    srcb, dstT, m0, k0 = _wT_jobs[i]
                    p = i % 2
                    if i < 2:
                        s.wait_ge(s_cp["w12"].h, 16)
                        s.wait_ge(s_cp["w3"].h, 16)
                    else:
                        s.wait_ge(s_wTw[p].h, s_wTw[p].n)
                    s_wTt[p].inc(s.dma_start(out=wT_buf[p][:, :],
                                             in_=srcb[m0:m0 + 512, k0:k0 + 128],
                                             transpose=True), 16)
                    s.wait_ge(s_wTt[p].h, s_wTt[p].n)
                    s_wTw[p].inc(s.dma_start(out=dstT[k0:k0 + 128, m0:m0 + 512],
                                             in_=wT_buf[p][:, :]), 16)
            blk.sync(go)

        s1 = ExitStack()
        with s1:
            xln1T = sb(s1, "xln1T", [128, 32 * TH], BF16)
            C["cos"] = sb(s1, "k_cos", [128, TH], BF16)
            C["sin"] = sb(s1, "k_sin", [128, TH], BF16)

            def ph0b(s):
                sem_c0.inc(s.dma_start(out=C["cos"][:, :], in_=ci["cos"][:, :]), 16)
                sem_c0.inc(s.dma_start(out=C["sin"][:, :], in_=ci["sin"][:, :]), 16)
                s.wait_ge(sem_c0.h, sem_c0.n)
            blk.sync(ph0b)

            def xt(kc):
                return xln1T[:, kc * TH:(kc + 1) * TH]

            wkv_sb = [[sb(s1, "wkv00", [128, DM], BF16), None, None],
                      [sb(s1, "wkv10", [128, DM], BF16), None, None]]
            g_wkv = [[Sem(octx, nc, f"gwkv0{j}") for j in range(3)],
                     [Sem(octx, nc, f"gwkv1{j}") for j in range(3)]]
            kv_free = {}
            g_tot = {}

            def emit_kv_load(hl):
                for w_i, wsrc in ((0, s_wkb), (1, s_wvb)):
                    stq = kv_free.get((w_i, hl % 3), 0)

                    def a1_w(s, hl=hl, w_i=w_i, wsrc=wsrc, stq=stq):
                        if hl <= 1:
                            s.wait_ge(s_cp["wk" if w_i == 0 else "wv"].h, 16)
                        if stq:
                            s.wait_ge(sT.h, stq)
                        for kcg in range(2):
                            g_wkv[w_i][hl % 3].inc(s.dma_start(
                                out=wkv_sb[w_i][hl % 3][:, kcg * 2048:(kcg + 1) * 2048]
                                .rearrange("p (kc j) -> p kc j", j=128),
                                in_=wsrc[kcg * 2048:(kcg + 1) * 2048, hl * 128:(hl + 1) * 128]
                                .rearrange("(kc p) j -> p kc j", p=128)), 16)
                    blk.scalar(a1_w)
                    g_tot[(w_i, hl)] = g_wkv[w_i][hl % 3].n

            s1a = ExitStack()
            s1a.__enter__()
            trash = sb(s1a, "p1trash", [128, DM], BF16)
            p1x = [sb(s1a, "p1x0", [128, DM], F32), sb(s1a, "p1x1", [128, DM], F32)]
            p1l = [sb(s1a, "p1l0", [128, DM], BF16), sb(s1a, "p1l1", [128, DM], BF16)]
            lx = [Sem(octx, nc, "lx0"), Sem(octx, nc, "lx1")]
            wx = Sem(octx, nc, "wx")
            tr_g = [Sem(octx, nc, f"tr{i}") for i in range(4)]

            for tc in range(9):
                p = tc % 2
                src = x_halo if tc == 0 else x_seg
                r0 = 0 if tc == 0 else (tc - 1) * 128
                svq = sV.n

                def p1_load(s, src=src, r0=r0, p=p, tc=tc, svq=svq):
                    if tc >= 2:
                        s.wait_ge(sV.h, svq)
                    lx[p].inc(s.dma_start(out=p1x[p][:, :], in_=src[r0:r0 + 128, :]), 16)
                blk.sync(p1_load)

                def p1_sv(v, p=p):
                    v.wait_ge(lx[p].h, lx[p].n)
                    sV.inc(v.reduce_sum(stt[p][:, 0:1], p1x[p][:, :], axis=mybir.AxisListType.X))
                V(p1_sv)

                def p1_ss(s, p=p):
                    s.wait_ge(lx[p].h, lx[p].n)
                    sS.inc(s.activation(trash[:, :], p1x[p][:, :], AF.Square, accum_out=stt[p][:, 1:2]))
                SC(p1_ss)
                svn, ssn = sV.n, sS.n

                def p1_mk(v, p=p, svn=svn, ssn=ssn):
                    v.wait_ge(sV.h, svn)
                    v.wait_ge(sS.h, ssn)
                    st = stt[p]
                    v.tensor_scalar(st[:, 2:3], st[:, 0:1], 1.0 / DM, None, ALU.mult)
                    v.tensor_scalar(st[:, 3:4], st[:, 1:2], 1.0 / DM, None, ALU.mult)
                    v.tensor_tensor(st[:, 4:5], st[:, 2:3], st[:, 2:3], ALU.mult)
                    sV.inc(v.tensor_tensor(st[:, 3:4], st[:, 3:4], st[:, 4:5], ALU.subtract))
                V(p1_mk)
                svn = sV.n

                def p1_sq(s, p=p, svn=svn):
                    s.wait_ge(sV.h, svn)
                    sS.inc(s.activation(stt[p][:, 4:5], stt[p][:, 3:4], AF.Sqrt, bias=EPS))
                SC(p1_sq)
                ssn = sS.n

                wxq = wx.n

                def p1_fin(v, p=p, ssn=ssn, wxq=wxq):
                    v.wait_ge(sS.h, ssn)
                    v.wait_ge(wx.h, wxq)
                    st = stt[p]
                    v.reciprocal(st[:, 5:6], st[:, 4:5])
                    v.tensor_tensor(st[:, 6:7], st[:, 2:3], st[:, 5:6], ALU.mult)
                    v.tensor_scalar(st[:, 6:7], st[:, 6:7], -1.0, None, ALU.mult)
                    sV.inc(v.tensor_scalar(p1l[p][:, :], p1x[p][:, :],
                                           st[:, 5:6], st[:, 6:7], ALU.mult, ALU.add))
                V(p1_fin)
                svn = sV.n

                def p1_st(s, tc=tc, p=p, svn=svn):
                    s.wait_ge(sV.h, svn)
                    wx.inc(s.dma_start(out=s_xln1[tc * 128:(tc + 1) * 128, :], in_=p1l[p][:, :]), 16)
                SC(p1_st)

            emit_kv_load(0)

            def p1_tr(s):
                s.wait_ge(wx.h, wx.n)
                for kc in range(32):
                    tr_g[kc // 8].inc(s.dma_start(out=xt(kc), in_=s_xln1[:, kc * 128:(kc + 1) * 128],
                                                  transpose=True), 16)
            blk.sync(p1_tr)

            tr_tot = [g.n for g in tr_g]

            def w_copies_late(g, tr_tot=tr_tot):
                for gg, tt in zip(tr_g, tr_tot):
                    g.wait_ge(gg.h, tt)
                for nm, srcT, dstT in [("w12", w12_sh, sh_w12b), ("w3", w3_sh, sh_w3b),
                                       ("wq", Wq_sh, sh_wq), ("wo", Wo_sh, sh_wo)]:
                    s_cp[nm].inc(g.dma_start(out=dstT[:, :], in_=srcT[:, :]), 16)
            blk.gpsimd(w_copies_late)

            def wait_xln1T(eng):
                for g in tr_g:
                    eng.wait_ge(g.h, g.n)

            s1a.__exit__(None, None, None)
            v_tok = sb(s1, "v_tok", [128, 9 * 1024], BF16)
            S_entry = sb(s1, "S_entry", [128, KVH * NCH * 128], BF16)
            expA = sb(s1, "expA", [128, KVH * NCH * 128], BF16)
            Dcum = sb(s1, "Dcum", [128, KVH * NCH], F32)
            S_run = sb(s1, "S_run", [128, 128], F32)
            logD = sb(s1, "logD", [128, 1], F32)
            S_in = sb(s1, "S_in", [128, KVH * 128], BF16)
            s1b = ExitStack()
            s1b.__enter__()
            k_feat = sb(s1b, "k_feat", [128, KVH * TH], BF16)
            sigk = sb(s1b, "sigk", [128, TH], BF16)
            v_feat = sb(s1b, "v_feat", [128, TH], BF16)
            gs_bf = [sb(s1b, "gs0", [128, 128], BF16), sb(s1b, "gs1", [128, 128], BF16)]
            gs_tok = [sb(s1b, "gst0", [128, 128], BF16), sb(s1b, "gst1", [128, 128], BF16)]
            ek_bf = sb(s1b, "ek", [128, 128], BF16)
            zkr = sb(s1b, "zkr", [1, 128], BF16)
            zkb = sb(s1b, "zkb", [128, 128], BF16)
            t1_bf = sb(s1b, "t1", [128, 128], BF16)
            exn_bf = sb(s1b, "exn", [128, 128], BF16)
            exb_bf = sb(s1b, "exb", [128, 128], BF16)
            kbar_bf = sb(s1b, "kbar", [128, 128], BF16)
            kbar_tok = sb(s1b, "kbart", [128, 128], BF16)
            dc_f32 = sb(s1b, "dcf", [128, 1], F32)
            khat = sb(s1b, "khat", [128, TH], BF16)
            ktil = sb(s1b, "ktil", [128, T], BF16)
            xr_bf = sb(s1b, "xr", [128, 384], BF16)
            sexit_bf = sb(s1b, "sexit", [128, 128], BF16)
            dseg_bf = sb(s1b, "dseg", [128, 1], BF16)
            wkk = Sem(octx, nc, "wkk")
            sem_ag = Sem(octx, nc, "agst")
            sem_cc = Sem(octx, nc, "cc")

            def kf(h):
                return k_feat[:, h * TH:(h + 1) * TH]

            def sg(h):
                return sigk

            def vt(ch):
                return v_tok[:, ch * 1024:(ch + 1) * 1024]

            def se(h, c):
                return S_entry[:, (h * NCH + c) * 128:(h * NCH + c + 1) * 128]

            def ea(h, c):
                return expA[:, (h * NCH + c) * 128:(h * NCH + c + 1) * 128]

            wkv_sb[0][1] = sb(s1b, "wkv01", [128, DM], BF16)
            wkv_sb[1][1] = sb(s1b, "wkv11", [128, DM], BF16)
            wkv_sb[0][2] = sb(s1b, "wkv02", [128, DM], BF16)
            wkv_sb[1][2] = sb(s1b, "wkv12", [128, DM], BF16)
            emit_kv_load(1)
            emit_kv_load(2)
            for h in range(KVH):
                for w_i in range(2):
                    for tt in range(3):
                        myps = PSROT[2 * w_i + tt % 2]
                        svq = sV.n
                        gq = g_tot[(w_i, h)]

                        def a1_mm(t, h=h, w_i=w_i, tt=tt, myps=myps, svq=svq, gq=gq):
                            t.wait_ge(g_wkv[w_i][h % 3].h, gq)
                            wait_xln1T(t)
                            t.wait_ge(sV.h, svq)
                            for kc in range(32):
                                ins = t.matmul(myps[:, 0:384],
                                               wkv_sb[w_i][h % 3][:, kc * 128:(kc + 1) * 128],
                                               xt(kc)[:, tt * 384:(tt + 1) * 384],
                                               start=(kc == 0), stop=(kc == 31))
                                if kc == 31:
                                    sT.inc(ins)
                        blk.tensor(a1_mm)
                        stn = sT.n

                        def a1_ev(v, h=h, w_i=w_i, tt=tt, myps=myps, stn=stn):
                            v.wait_ge(sT.h, stn)
                            dst = kf(h) if w_i == 0 else v_feat
                            sV.inc(v.tensor_copy(dst[:, tt * 384:(tt + 1) * 384], myps[:, 0:384]))
                        V(a1_ev)
                    kv_free[(w_i, h % 3)] = sT.n
                if h + 3 < KVH:
                    emit_kv_load(h + 3)

                for ch in range(9):
                    myt = psT if ch % 2 == 0 else psT2
                    svn = sV.n

                    def a1_tr(t, ch=ch, myt=myt, svn=svn):
                        t.wait_ge(sV.h, svn)
                        sT.inc(t.transpose(myt[:, :], v_feat[:, ch * 128:(ch + 1) * 128], C["ident"][:, :]))
                    blk.tensor(a1_tr)
                    stn = sT.n

                    def a1_te(v, ch=ch, h=h, myt=myt, stn=stn):
                        v.wait_ge(sT.h, stn)
                        sV.inc(v.tensor_copy(vt(ch)[:, h * 128:(h + 1) * 128], myt[:, :]))
                    V(a1_te)
                emit_w_chunks(8)

            for h in range(KVH):
                hp = h % 2
                svq_sig = sV.n

                def a2_sig(s, h=h, svq_sig=svq_sig):
                    s.wait_ge(sV.h, svq_sig)
                    sS.inc(s.activation(sigk[:, :], kf(h)[:, :], AF.Sigmoid))
                SC(a2_sig)

                ssq_i = sS.n

                def a3_init(v, h=h, ssq_i=ssq_i):
                    v.wait_ge(sS.h, ssq_i)
                    v.memset(S_run[:, :], 0.0)
                    v.memset(logD[:, :], 0.0)
                    sV.inc(v.memset(se(h, 0)[:, :], 0.0))
                V(a3_init)

                for tt in range(3):
                    svn = sV.n

                    def a3_rot(t, h=h, tt=tt, svn=svn):
                        t.wait_ge(sV.h, svn)
                        if tt == 0:
                            t.wait_ge(wkk.h, wkk.n)
                        sT.inc(t.matmul(psS[:, 0:384], C["rot"][:, :],
                                        kf(h)[:, tt * 384:(tt + 1) * 384], start=True, stop=True))
                    blk.tensor(a3_rot)
                    stn = sT.n

                    def a3_rope(v, h=h, hp=hp, tt=tt, stn=stn):
                        v.wait_ge(sT.h, stn)
                        sl = slice(tt * 384, (tt + 1) * 384)
                        v.tensor_copy(xr_bf[:, :], psS[:, 0:384])
                        v.tensor_tensor(khat[:, sl], kf(h)[:, sl], C["cos"][:, sl], ALU.mult)
                        v.tensor_tensor(xr_bf[:, :], xr_bf[:, :], C["sin"][:, sl], ALU.mult)
                        sV.inc(v.tensor_tensor(khat[:, sl], khat[:, sl], xr_bf[:, :], ALU.add))
                    V(a3_rope)

                for c in range(NCH):
                    cp = c % 2
                    myt = psT if cp == 0 else psT2
                    myt2 = psT2 if cp == 0 else psT
                    svn = sV.n

                    def a3_ln(s, h=h, c=c, cp=cp, svn=svn):
                        s.wait_ge(sV.h, svn)
                        sS.inc(s.activation(gs_bf[cp][:, :],
                                            sg(h)[:, 128 + c * 128:128 + (c + 1) * 128], AF.Ln))
                    SC(a3_ln)
                    ssn = sS.n

                    def a3_gst(t, cp=cp, myt=myt, ssn=ssn):
                        t.wait_ge(sS.h, ssn)
                        sT.inc(t.transpose(myt[:, :], gs_bf[cp][:, :], C["ident"][:, :]))
                    blk.tensor(a3_gst)
                    stn = sT.n

                    def a3_gse(v, cp=cp, myt=myt, stn=stn):
                        v.wait_ge(sT.h, stn)
                        sV.inc(v.tensor_copy(gs_tok[cp][:, :], myt[:, :]))
                    V(a3_gse)
                    svn = sV.n

                    def a3_amm(t, cp=cp, svn=svn, ssn=ssn):
                        t.wait_ge(sV.h, svn)
                        t.wait_ge(sS.h, sS.n)
                        t.matmul(psS[:, 0:128], gs_tok[cp][:, :], C["lt"][:, :], start=True, stop=True)
                        sT.inc(t.matmul(psS[:, 128:256], gs_tok[cp][:, :], C["uts"][:, :],
                                        start=True, stop=True))
                    blk.tensor(a3_amm)
                    stn = sT.n

                    def a3_ex(s, h=h, c=c, stn=stn):
                        s.wait_ge(sT.h, stn)
                        s.wait_ge(sV.h, sV.n)
                        s.activation(Dcum[:, h * NCH + c:h * NCH + c + 1], logD[:, :], AF.Exp)
                        s.activation(ea(h, c)[:, :], psS[:, 0:128], AF.Exp)
                        s.activation(dc_f32[:, :], psS[:, 127:128], AF.Exp)
                        s.activation(exn_bf[:, :], psS[:, 0:128], AF.Exp, scale=-1.0)
                        s.activation(exb_bf[:, :], psS[:, 128:256], AF.Exp)
                        sS.inc(s.activation(ek_bf[:, :], kf(h)[:, 128 + c * 128:128 + (c + 1) * 128],
                                            AF.Exp, bias=SHIFT))
                    SC(a3_ex)
                    ssn = sS.n

                    def a3_ld(v, ssn=ssn):
                        v.wait_ge(sS.h, ssn)
                        sV.inc(v.tensor_tensor(logD[:, :], logD[:, :], psS[:, 127:128], ALU.add))
                    V(a3_ld)
                    svn = sV.n

                    def a3_zk(t, ssn=ssn, svn=svn):
                        t.wait_ge(sS.h, ssn)
                        t.wait_ge(sV.h, svn)
                        sT.inc(t.matmul(psZ[0:1, 0:128], C["ones"][:, :], ek_bf[:, :],
                                        start=True, stop=True))
                    blk.tensor(a3_zk)
                    stn = sT.n

                    def a3_zr(v, stn=stn):
                        v.wait_ge(sT.h, stn)
                        sV.inc(v.reciprocal(zkr[:, :], psZ[0:1, 0:128]))
                    V(a3_zr)
                    svn = sV.n

                    def a3_zb(t, svn=svn):
                        t.wait_ge(sV.h, svn)
                        sT.inc(t.matmul(psZ[:, 128:256], C["onerow"][:, :], zkr[:, :],
                                        start=True, stop=True))
                    blk.tensor(a3_zb)
                    stn = sT.n

                    def a3_kt(v, h=h, hp=hp, c=c, stn=stn):
                        v.wait_ge(sT.h, stn)
                        v.tensor_copy(zkb[:, :], psZ[:, 128:256])
                        v.tensor_tensor(t1_bf[:, :], ek_bf[:, :], zkb[:, :], ALU.mult)
                        v.tensor_tensor(ktil[:, c * 128:(c + 1) * 128], t1_bf[:, :],
                                        exn_bf[:, :], ALU.mult)
                        sV.inc(v.tensor_tensor(kbar_bf[:, :], t1_bf[:, :], exb_bf[:, :], ALU.mult))
                    V(a3_kt)
                    svn = sV.n

                    def a3_kbt(t, myt2=myt2, svn=svn):
                        t.wait_ge(sV.h, svn)
                        sT.inc(t.transpose(myt2[:, :], kbar_bf[:, :], C["ident"][:, :]))
                    blk.tensor(a3_kbt)
                    stn = sT.n

                    def a3_kbe(v, myt2=myt2, stn=stn):
                        v.wait_ge(sT.h, stn)
                        sV.inc(v.tensor_copy(kbar_tok[:, :], myt2[:, :]))
                    V(a3_kbe)
                    svn = sV.n

                    def a3_kv(t, h=h, c=c, svn=svn):
                        t.wait_ge(sV.h, svn)
                        sT.inc(t.matmul(psS[:, 256:384], kbar_tok[:, :],
                                        vt(c + 1)[:, h * 128:(h + 1) * 128], start=True, stop=True))
                    blk.tensor(a3_kv)
                    stn = sT.n

                    agq = sem_ag.n

                    def a3_su(v, h=h, c=c, stn=stn, ssn=ssn, agq=agq):
                        v.wait_ge(sT.h, stn)
                        v.wait_ge(sS.h, ssn)
                        if c == NCH - 1:
                            v.wait_ge(sem_ag.h, agq)
                        v.scalar_tensor_tensor(S_run[:, :], S_run[:, :], dc_f32[:, 0:1],
                                               psS[:, 256:384], ALU.mult, ALU.add)
                        dst = se(h, c + 1) if c < NCH - 1 else sexit_bf
                        sV.inc(v.tensor_copy(dst[:, :], S_run[:, :]))
                    V(a3_su)

                svn, ssn0 = sV.n, sS.n

                agq2 = sem_ag.n

                def a3_ds(s, svn=svn, agq2=agq2):
                    s.wait_ge(sV.h, svn)
                    s.wait_ge(sem_ag.h, agq2)
                    sS.inc(s.activation(dseg_bf[:, :], logD[:, :], AF.Exp))
                SC(a3_ds)
                ssn = sS.n

                def a3_ag(s, h=h, hp=hp, ssn=ssn, svn=svn):
                    s.wait_ge(sS.h, ssn)
                    s.wait_ge(sV.h, svn)
                    sem_ag.inc(s.dma_start(out=ag_in[:, h * 129:h * 129 + 128], in_=sexit_bf[:, :]), 16)
                    sem_ag.inc(s.dma_start(out=ag_in[:, h * 129 + 128:h * 129 + 129],
                                           in_=dseg_bf[:, :],
                                           allow_slow_non_contiguous=True), 16)
                    wkk.inc(s.dma_start(out=s_khat[h * 128:(h + 1) * 128, :], in_=khat[:, :]), 16)
                    wkk.inc(s.dma_start(out=s_ktil[h * 128:(h + 1) * 128, :], in_=ktil[:, :]), 16)
                SC(a3_ag)
                emit_w_chunks(4)
                if h == 5:
                    blk.gpsimd(w_ags_pre)

            def a4(g):
                g.wait_ge(sem_ag.h, sem_ag.n)
                sem_cc.inc(g.collective_compute(
                    "AllGather", ALU.bypass, replica_groups=[list(range(NCORES))],
                    ins=[ag_in[:, :]], outs=[ag_out[:, :]]), 1)
            blk.gpsimd(a4)

            assert _wT_state["i"] == len(_wT_jobs)
            wTw_tot = (s_wTw[0].n, s_wTw[1].n)

            def w_ags_post(g):
                g.wait_ge(s_wTw[0].h, wTw_tot[0])
                g.wait_ge(s_wTw[1].h, wTw_tot[1])
                ag_w["w12"].inc(g.collective_compute(
                    "AllGather", ALU.bypass, replica_groups=[list(range(NCORES))],
                    ins=[sh_w12T[:, :]], outs=[s_w12T[:, :]]), 1)
                ag_w["w3"].inc(g.collective_compute(
                    "AllGather", ALU.bypass, replica_groups=[list(range(NCORES))],
                    ins=[sh_w3T[:, :]], outs=[s_w3T[:, :]]), 1)
            blk.gpsimd(w_ags_post)

            s1b.__exit__(None, None, None)
            wkk_tot, ag_tot = wkk.n, sem_ag.n
            for _f in (
                lambda e: (e.wait_ge(wkk.h, wkk_tot), e.wait_ge(sem_ag.h, ag_tot)),
            ):
                V(_f)
                SC(_f)
                blk.sync(_f)
                blk.gpsimd(_f)
            s1c = ExitStack()
            s1c.__enter__()
            wq_sb = [[sb(s1c, "wq00", [128, 2048], BF16), sb(s1c, "wq01", [128, 2048], BF16)],
                     [sb(s1c, "wq10", [128, 2048], BF16), sb(s1c, "wq11", [128, 2048], BF16)]]
            g_wq = [[Sem(octx, nc, f"gwq0{j}") for j in range(2)],
                    [Sem(octx, nc, f"gwq1{j}") for j in range(2)]]
            l_kk = [Sem(octx, nc, "lkk0"), Sem(octx, nc, "lkk1")]
            khat_r = [sb(s1c, "khr0", [128, TH], BF16), sb(s1c, "khr1", [128, TH], BF16)]
            ktil_r = [sb(s1c, "ktr0", [128, T], BF16), sb(s1c, "ktr1", [128, T], BF16)]

            def pb_pre(s):
                s.wait_ge(wkk.h, wkk.n)
                l_kk[0].inc(s.dma_start(out=khat_r[0][:, :], in_=s_khat[0:128, :]), 16)
                l_kk[0].inc(s.dma_start(out=ktil_r[0][:, :], in_=s_ktil[0:128, :]), 16)
                s.wait_ge(ag_w["wq"].h, 1)
                for mc in range(2):
                    for kh in range(2):
                        g_wq[mc][kh].inc(s.dma_start(
                            out=wq_sb[mc][kh][:, :]
                            .rearrange("p (kc j) -> p kc j", j=128),
                            in_=s_wqb[kh * 2048:(kh + 1) * 2048, mc * 128:(mc + 1) * 128]
                            .rearrange("(kc p) j -> p kc j", p=128)), 16)
            blk.sync(pb_pre)
            tmp_pc = sb(s1c, "tmp_pc", [128, 128], BF16)
            exp_q = sb(s1c, "exp_q", [128, T], BF16)
            q_bf = sb(s1c, "q_bf", [128, 512], BF16)
            xrq_bf = sb(s1c, "xrq", [128, 512], BF16)
            qhat = sb(s1c, "qhat", [128, 4 * T], BF16)
            qtil = sb(s1c, "qtil", [128, 4 * T], BF16)
            zqr = sb(s1c, "zqr", [1, 512], BF16)
            zqb = sb(s1c, "zqb", [128, 512], BF16)
            qt_tmp = sb(s1c, "qt_tmp", [128, 512], BF16)
            s1c2 = ExitStack()
            s1c2.__enter__()
            dsg8 = sb(s1c2, "dsg8", [128, 8], F32)
            P_all = sb(s1c2, "P_all", [128, KVH * 128], BF16)
            tmp_pc8 = sb(s1c2, "tmp_pc8", [128, KVH * 128], BF16)
            ag_all = sb(s1c2, "ag_all", [128, 4 * 1032], BF16)
            lag = [Sem(octx, nc, "lag0"), Sem(octx, nc, "lag1")]

            sv_pc = [0]
            for gh in range(2):
                svq_pc = sv_pc[0]

                def pc_load(s, gh=gh, svq_pc=svq_pc):
                    if gh == 0:
                        s.wait_ge(sem_cc.h, sem_cc.n)
                    else:
                        s.wait_ge(sV.h, svq_pc)
                    lag[gh].inc(s.dma_start(
                        out=ag_all[:, :].rearrange("p (r c) -> p r c", c=1032),
                        in_=ag_out[gh * 512:(gh + 1) * 512, :]
                        .rearrange("(r p) c -> p r c", p=128)), 16)
                blk.sync(pc_load)

                def pc_comb(v, gh=gh):
                    v.wait_ge(lag[gh].h, lag[gh].n)
                    for rr in range(4):
                        r = gh * 4 + rr
                        agr = ag_all[:, rr * 1032:(rr + 1) * 1032]
                        if r == 0:
                            v.memset(P_all[:, :], 0.0)
                            v.memset(S_in[:, :], 0.0)
                        if r == 4:
                            v.memset(P_all[:, :], 0.0)
                        v.tensor_scalar(tmp_pc8[:, :], P_all[:, :], C["eoh"][:, r:r + 1],
                                        None, ALU.mult)
                        v.tensor_tensor(S_in[:, :], S_in[:, :], tmp_pc8[:, :], ALU.add)
                        v.tensor_copy(dsg8[:, :],
                                      agr.rearrange("p (h c) -> p h c", c=129)[:, :, 128:129])
                        for h in range(KVH):
                            v.scalar_tensor_tensor(P_all[:, h * 128:(h + 1) * 128],
                                                   P_all[:, h * 128:(h + 1) * 128],
                                                   dsg8[:, h:h + 1],
                                                   agr[:, h * 129:h * 129 + 128],
                                                   ALU.mult, ALU.add)
                    sV.inc(v.memset(tmp_pc[:, 0:1], 0.0))
                V(pc_comb)
                sv_pc[0] = sV.n
            s1c2.__exit__(None, None, None)

            w_os = Sem(octx, nc, "w_os")
            eT = [sb(s1c, f"eT{i}", [128, 256], BF16) for i in range(3)]
            zswr = sb(s1c, "zswr", [1, 512], BF16)
            zswb = sb(s1c, "zswb", [128, 512], F32)
            pt_bf = sb(s1c, "pt_bf", [128, 512], BF16)
            tswa = sb(s1c, "tswa", [128, 512], F32)
            o_buf = sb(s1c, "o_buf", [128, 4 * T], BF16)
            wq_free = {}
            g_tot_wq = {(0, 0): 16, (0, 1): 32, (1, 0): 16, (1, 1): 32}
            kk_free = [0, 0]
            ps_free = {}
            qt_end = [0]

            def qhat_ap(j):
                return qhat[:, :].rearrange("p (h t) -> p h t", h=4)[:, :, j * 64:(j + 1) * 64]

            def qtil_ap(c):
                return qtil[:, :].rearrange("p (h j i) -> p j h i", h=4, j=16)[:, 2 * c:2 * c + 2, :, :]

            for h in range(KVH):
                hp = h % 2

                kkq = kk_free[hp]

                def b_kk(s, h=h, hp=hp, kkq=kkq):
                    if kkq:
                        s.wait_ge(sT.h, kkq)
                    l_kk[hp].inc(s.dma_start(out=khat_r[hp][:, :],
                                             in_=s_khat[h * 128:(h + 1) * 128, :]), 16)
                    l_kk[hp].inc(s.dma_start(out=ktil_r[hp][:, :],
                                             in_=s_ktil[h * 128:(h + 1) * 128, :]), 16)
                if h >= 1:
                    blk.sync(b_kk)

                for mc in range(4):
                    myps = [PSROT[(2 * mc + tt) % 4] for tt in range(2)]
                    gm = h * 4 + mc
                    pair = gm % 2
                    for kh in range(2):
                        stq = wq_free.get(pair, 0)

                        def b_wq(s, h=h, mc=mc, kh=kh, pair=pair, stq=stq):
                            if stq:
                                s.wait_ge(sT.h, stq)
                            g_wq[pair][kh].inc(s.dma_start(
                                out=wq_sb[pair][kh][:, :]
                                .rearrange("p (kc j) -> p kc j", j=128),
                                in_=s_wqb[kh * 2048:(kh + 1) * 2048,
                                          (4 * h + mc) * 128:(4 * h + mc + 1) * 128]
                                .rearrange("(kc p) j -> p kc j", p=128)), 16)
                        if gm >= 2:
                            blk.sync(b_wq)
                        g_tot_wq[(gm, kh)] = g_wq[pair][kh].n

                    stn_done = {}
                    for kh in range(2):
                        for tt in range(2):
                            idx = (2 * mc + tt) % 4
                            fr_s, fr_v = ps_free.get(idx, (0, 0))
                            gq = g_tot_wq[(gm, kh)]

                            def b_qmm(t, kh=kh, tt=tt, pair=pair, myps=myps,
                                      fr_s=fr_s, fr_v=fr_v, gq=gq):
                                t.wait_ge(g_wq[pair][kh].h, gq)
                                if kh == 0:
                                    t.wait_ge(sS.h, fr_s)
                                    t.wait_ge(sV.h, fr_v)
                                for kc in range(16):
                                    ins = t.matmul(myps[tt][:, :],
                                                   wq_sb[pair][kh][:, kc * 128:(kc + 1) * 128],
                                                   xt(kh * 16 + kc)[:, 128 + tt * 512:128 + (tt + 1) * 512],
                                                   start=(kh == 0 and kc == 0),
                                                   stop=(kh == 1 and kc == 15))
                                    if kc == 15:
                                        sT.inc(ins)
                            blk.tensor(b_qmm)
                            if kh == 1:
                                stn_done[tt] = sT.n
                        wq_free[pair] = sT.n

                    for tt in range(2):
                        stn = stn_done[tt]
                        ps_tt = myps[tt]
                        qtp = qt_end[0]

                        def b_qe(s, tt=tt, ps_tt=ps_tt, stn=stn, qtp=qtp):
                            s.wait_ge(sT.h, stn)
                            s.wait_ge(sV.h, qtp)
                            sS.inc(s.activation(exp_q[:, tt * 512:(tt + 1) * 512],
                                                ps_tt[:, :], AF.Exp, bias=SHIFT))
                        SC(b_qe)

                        def b_qb(v, ps_tt=ps_tt, stn=stn):
                            v.wait_ge(sT.h, stn)
                            sV.inc(v.tensor_copy(q_bf[:, :], ps_tt[:, :]))
                        V(b_qb)
                        ps_free[(2 * mc + tt) % 4] = (sS.n, sV.n)
                        svn = sV.n

                        def b_qr(t, svn=svn):
                            t.wait_ge(sV.h, svn)
                            sT.inc(t.matmul(psS[:, :], C["rot"][:, :], q_bf[:, :], start=True, stop=True))
                        blk.tensor(b_qr)
                        stn2 = sT.n

                        def b_qro(v, mc=mc, tt=tt, stn2=stn2):
                            v.wait_ge(sT.h, stn2)
                            sl = slice(128 + tt * 512, 128 + (tt + 1) * 512)
                            dsl = slice(mc * T + tt * 512, mc * T + (tt + 1) * 512)
                            v.tensor_copy(xrq_bf[:, :], psS[:, :])
                            v.tensor_tensor(qhat[:, dsl], q_bf[:, :], C["cos"][:, sl], ALU.mult)
                            v.tensor_tensor(xrq_bf[:, :], xrq_bf[:, :], C["sin"][:, sl], ALU.mult)
                            sV.inc(v.tensor_tensor(qhat[:, dsl], qhat[:, dsl], xrq_bf[:, :], ALU.add))
                        V(b_qro)

                    for tt in range(2):
                        ssn = sS.n

                        def b_zq(t, tt=tt, ssn=ssn):
                            t.wait_ge(sS.h, ssn)
                            t.wait_ge(sV.h, sV.n)
                            sT.inc(t.matmul(psZ[0:1, 0:512], C["ones"][:, :],
                                            exp_q[:, tt * 512:(tt + 1) * 512],
                                            start=True, stop=True))
                        blk.tensor(b_zq)
                        stn = sT.n

                        def b_zqr(v, stn=stn):
                            v.wait_ge(sT.h, stn)
                            sV.inc(v.reciprocal(zqr[:, :], psZ[0:1, 0:512]))
                        V(b_zqr)
                        svn = sV.n

                        def b_zqb(t, svn=svn):
                            t.wait_ge(sV.h, svn)
                            sT.inc(t.matmul(psZ[:, 0:512], C["half"][:, :], zqr[:, :],
                                            start=True, stop=True))
                        blk.tensor(b_zqb)
                        stn = sT.n

                        def b_qt(v, h=h, mc=mc, tt=tt, stn=stn):
                            v.wait_ge(sT.h, stn)
                            v.tensor_copy(zqb[:, :], psZ[:, 0:512])
                            v.tensor_tensor(qt_tmp[:, :],
                                            exp_q[:, tt * 512:(tt + 1) * 512],
                                            zqb[:, :], ALU.mult)
                            for cc_ in range(4):
                                c = tt * 4 + cc_
                                sV.inc(v.tensor_tensor(
                                    qtil[:, mc * T + c * 128:mc * T + (c + 1) * 128],
                                    qt_tmp[:, cc_ * 128:(cc_ + 1) * 128], ea(h, c)[:, :], ALU.mult))
                        V(b_qt)
                        qt_end[0] = sV.n

                svn, stn0 = sV.n, sT.n

                def b_sg(v, h=h, svn=svn, stn0=stn0):
                    v.wait_ge(sV.h, svn)
                    v.wait_ge(sT.h, stn0)
                    v.wait_ge(w_os.h, w_os.n)
                    for c in range(NCH):
                        v.scalar_tensor_tensor(se(h, c)[:, :],
                                               S_in[:, h * 128:(h + 1) * 128],
                                               Dcum[:, h * NCH + c:h * NCH + c + 1],
                                               se(h, c)[:, :], ALU.mult, ALU.add)
                    sV.inc(v.memset(tmp_pc[:, 0:1], 0.0))
                V(b_sg)

                for c in range(NCH):
                    svq = sV.n

                    def b_sc(t, h=h, hp=hp, c=c, svq=svq):
                        t.wait_ge(l_kk[hp].h, l_kk[hp].n)
                        t.wait_ge(sV.h, svq)
                        t.wait_ge(sS.h, sS.n)
                        j = 2 * c
                        t.matmul(psS[:, 0:256], khat_r[hp][:, c * 128:(c + 1) * 128],
                                 qhat_ap(j), start=True, stop=True)
                        t.matmul(psS[:, 256:512], khat_r[hp][:, (c + 1) * 128:(c + 2) * 128],
                                 qhat_ap(j), start=True, stop=True)
                        sT.inc(t.matmul(psZ[:, 0:256], khat_r[hp][:, (c + 1) * 128:(c + 2) * 128],
                                        qhat_ap(j + 1), start=True, stop=True))
                    blk.tensor(b_sc)
                    stn = sT.n

                    def b_se(s, c=c, stn=stn):
                        s.wait_ge(sT.h, stn)
                        s.activation(eT[0][:, :], psS[:, 0:256], AF.Exp, scale=ISQD, bias=SHIFT)
                        s.activation(eT[1][:, :], psS[:, 256:512], AF.Exp, scale=ISQD, bias=SHIFT)
                        sS.inc(s.activation(eT[2][:, :], psZ[:, 0:256], AF.Exp, scale=ISQD, bias=SHIFT))
                    SC(b_se)
                    ssn = sS.n

                    def b_sm(v, c=c, ssn=ssn):
                        v.wait_ge(sS.h, ssn)
                        mA = C["swevA1"] if c == 0 else C["swevA"]
                        v.tensor_tensor(eT[0][:, :], eT[0][:, :], mA[:, :], ALU.mult)
                        v.tensor_tensor(eT[1][:, :], eT[1][:, :], C["swevB"][:, :], ALU.mult)
                        sV.inc(v.tensor_tensor(eT[2][:, :], eT[2][:, :], C["swodd"][:, :], ALU.mult))
                    V(b_sm)
                    svn = sV.n

                    def b_sz(t, h=h, c=c, svn=svn):
                        t.wait_ge(sV.h, svn)
                        t.matmul(psZ[0:1, 256:512], C["ones"][:, :], eT[0][:, :], start=True, stop=False)
                        t.matmul(psZ[0:1, 256:512], C["ones"][:, :], eT[1][:, :], start=False, stop=True)
                        t.matmul(psS[0:1, 0:256], C["ones"][:, :], eT[2][:, :], start=True, stop=True)
                        t.matmul(ps0[:, 0:256], vt(c)[:, h * 128:(h + 1) * 128], eT[0][:, :],
                                 start=True, stop=False)
                        t.matmul(ps0[:, 0:256], vt(c + 1)[:, h * 128:(h + 1) * 128], eT[1][:, :],
                                 start=False, stop=True)
                        sT.inc(t.matmul(ps0[:, 256:512], vt(c + 1)[:, h * 128:(h + 1) * 128], eT[2][:, :],
                                        start=True, stop=True))
                    blk.tensor(b_sz)
                    stn = sT.n

                    def b_szr(v, stn=stn):
                        v.wait_ge(sT.h, stn)
                        v.reciprocal(zswr[0:1, 0:256], psZ[0:1, 256:512])
                        sV.inc(v.reciprocal(zswr[0:1, 256:512], psS[0:1, 0:256]))
                    V(b_szr)
                    svn = sV.n

                    def b_szb(t, svn=svn):
                        t.wait_ge(sV.h, svn)
                        sT.inc(t.matmul(ps1[:, :], C["half"][:, :], zswr[0:1, :], start=True, stop=True))
                    blk.tensor(b_szb)
                    stn = sT.n

                    def b_sn(v, stn=stn):
                        v.wait_ge(sT.h, stn)
                        v.tensor_copy(zswb[:, :], ps1[:, :])
                        sV.inc(v.tensor_tensor(tswa[:, :], ps0[:, :], zswb[:, :], ALU.mult))
                    V(b_sn)
                    svn = sV.n

                    def b_pt(t, hp=hp, c=c, svn=svn):
                        t.wait_ge(sV.h, svn)
                        sT.inc(t.matmul(ps2[:, :], ktil_r[hp][:, c * 128:(c + 1) * 128],
                                        qtil_ap(c), start=True, stop=True))
                    blk.tensor(b_pt)
                    stn = sT.n

                    def b_pm(v, stn=stn):
                        v.wait_ge(sT.h, stn)
                        sV.inc(v.tensor_tensor(pt_bf[:, :], ps2[:, :], C["glam"][:, :], ALU.mult))
                    V(b_pm)
                    svn = sV.n

                    def b_go(t, h=h, c=c, svn=svn):
                        t.wait_ge(sV.h, svn)
                        t.matmul(ps3[:, :], vt(c + 1)[:, h * 128:(h + 1) * 128], pt_bf[:, :],
                                 start=True, stop=False)
                        sT.inc(t.matmul(ps3[:, :], se(h, c)[:, :], qtil_ap(c),
                                        start=False, stop=True))
                    blk.tensor(b_go)
                    stn = sT.n

                    def b_cb(v, c=c, stn=stn):
                        v.wait_ge(sT.h, stn)
                        tsw = tswa[:, :].rearrange("p (j h i) -> p h j i", j=2, h=4)
                        og = ps3[:, :].rearrange("p (j h i) -> p h j i", j=2, h=4)
                        for mc in range(4):
                            sV.inc(v.tensor_tensor(
                                o_buf[:, mc * T + c * 128:mc * T + (c + 1) * 128]
                                .rearrange("p (j i) -> p j i", j=2),
                                tsw[:, mc], og[:, mc], ALU.add))
                    V(b_cb)

                svn = sV.n

                def b_os(s, h=h, svn=svn):
                    s.wait_ge(sV.h, svn)
                    for mc in range(4):
                        w_os.inc(s.dma_start(out=s_o[(4 * h + mc) * 128:(4 * h + mc + 1) * 128, :],
                                             in_=o_buf[:, mc * T:(mc + 1) * T]), 16)
                SC(b_os)
                kk_free[hp] = sT.n
                emit_w_chunks(8)

            s1c.__exit__(None, None, None)
            wos_total = w_os.n

        s2 = ExitStack()
        with s2:

            s2b = ExitStack()
            s2b.__enter__()
            trash6 = sb(s2b, "trash6", [128, DM], BF16)
            atok = [sb(s2b, "atok0", [128, DM], BF16), sb(s2b, "atok1", [128, DM], BF16)]
            xrb0 = sb(s2b, "xrb0", [128, DM], F32)
            xrb = [xrb0, xrb0]
            x2b = [sb(s2b, "x2b0", [128, DM], F32), sb(s2b, "x2b1", [128, DM], F32)]
            ln3b = [sb(s2b, "ln3b0", [128, DM], BF16), sb(s2b, "ln3b1", [128, DM], BF16)]
            tat = [Sem(octx, nc, "tat0"), Sem(octx, nc, "tat1")]
            lx2 = [Sem(octx, nc, "lx20"), Sem(octx, nc, "lx21")]
            wx2 = [Sem(octx, nc, "wx2a"), Sem(octx, nc, "wx2b")]
            wl3 = [Sem(octx, nc, "wl3a"), Sem(octx, nc, "wl3b")]
            w_att = [Sem(octx, nc, "w_atta"), Sem(octx, nc, "w_attb")]
            tr3h = [Sem(octx, nc, "tr3h0"), Sem(octx, nc, "tr3h1")]
            wx2_hist, wl3_hist, sv_x2_hist = [], [], []
            q6_state = {}

            def emit_q6_tile(tc):
                p = tc % 2
                svq = sv_x2_hist[tc - 2] if tc >= 2 else 0

                def q6_ld(s, tc=tc, p=p, svq=svq):
                    if tc < 4:
                        s.wait_ge(w_att[0].h, q6_state["half0"])
                    elif tc == 4:
                        s.wait_ge(w_att[1].h, q6_state["half1"])
                    if svq:
                        s.wait_ge(sV.h, svq)
                    tat[p].inc(s.dma_start(out=atok[p][:, :],
                                           in_=s_att[:, tc * 128:(tc + 1) * 128],
                                           transpose=True), 16)
                blk.sync(q6_ld)
                xq = sv_x2_hist[tc - 1] if tc >= 1 else 0

                def q6_ldx(s, tc=tc, p=p, xq=xq):
                    if xq:
                        s.wait_ge(sV.h, xq)
                    lx2[p].inc(s.dma_start(out=xrb[p][:, :], in_=x_seg[tc * 128:(tc + 1) * 128, :]), 16)
                blk.sync(q6_ldx)

                def q6_cv(v, p=p):
                    v.wait_ge(tat[p].h, tat[p].n)
                    sV.inc(v.reduce_sum(stt[p][:, 0:1], atok[p][:, :], axis=mybir.AxisListType.X))
                V(q6_cv)
                svn = sV.n

                def q6_ss(s, p=p, svn=svn):
                    s.wait_ge(sV.h, svn)
                    sS.inc(s.activation(trash6[:, :], atok[p][:, :], AF.Square, accum_out=stt[p][:, 1:2]))
                SC(q6_ss)
                ssn = sS.n

                def q6_mk(v, p=p, ssn=ssn):
                    v.wait_ge(sS.h, ssn)
                    st = stt[p]
                    v.tensor_scalar(st[:, 2:3], st[:, 0:1], 1.0 / DM, None, ALU.mult)
                    v.tensor_scalar(st[:, 3:4], st[:, 1:2], 1.0 / DM, None, ALU.mult)
                    v.tensor_tensor(st[:, 4:5], st[:, 2:3], st[:, 2:3], ALU.mult)
                    sV.inc(v.tensor_tensor(st[:, 3:4], st[:, 3:4], st[:, 4:5], ALU.subtract))
                V(q6_mk)
                svn = sV.n

                def q6_sq(s, p=p, svn=svn):
                    s.wait_ge(sV.h, svn)
                    sS.inc(s.activation(stt[p][:, 4:5], stt[p][:, 3:4], AF.Sqrt, bias=EPS))
                SC(q6_sq)
                ssn = sS.n

                wx2q = wx2_hist[tc - 2] if tc >= 2 else 0

                def q6_x2(v, p=p, ssn=ssn, wx2q=wx2q):
                    v.wait_ge(sS.h, ssn)
                    v.wait_ge(lx2[p].h, lx2[p].n)
                    if wx2q:
                        v.wait_ge(wx2[p].h, wx2q)
                    st = stt[p]
                    v.reciprocal(st[:, 5:6], st[:, 4:5])
                    v.tensor_tensor(st[:, 6:7], st[:, 2:3], st[:, 5:6], ALU.mult)
                    v.tensor_scalar(st[:, 6:7], st[:, 6:7], -1.0, None, ALU.mult)
                    v.tensor_scalar(x2b[p][:, :], atok[p][:, :], st[:, 5:6], st[:, 6:7],
                                    ALU.mult, ALU.add)
                    sV.inc(v.tensor_tensor(x2b[p][:, :], x2b[p][:, :], xrb[p][:, :], ALU.add))
                V(q6_x2)
                sv_x2_hist.append(sV.n)
                svn = sV.n

                def q6_sv3(v, p=p, svn=svn):
                    sV.inc(v.reduce_sum(stt[p][:, 0:1], x2b[p][:, :], axis=mybir.AxisListType.X))
                V(q6_sv3)
                svn2 = sV.n

                def q6_ss3(s, p=p, svn2=svn2):
                    s.wait_ge(sV.h, svn2)
                    sS.inc(s.activation(trash6[:, :], x2b[p][:, :], AF.Square, accum_out=stt[p][:, 1:2]))
                SC(q6_ss3)
                ssn = sS.n

                def q6_mk3(v, p=p, ssn=ssn):
                    v.wait_ge(sS.h, ssn)
                    st = stt[p]
                    v.tensor_scalar(st[:, 2:3], st[:, 0:1], 1.0 / DM, None, ALU.mult)
                    v.tensor_scalar(st[:, 3:4], st[:, 1:2], 1.0 / DM, None, ALU.mult)
                    v.tensor_tensor(st[:, 4:5], st[:, 2:3], st[:, 2:3], ALU.mult)
                    sV.inc(v.tensor_tensor(st[:, 3:4], st[:, 3:4], st[:, 4:5], ALU.subtract))
                V(q6_mk3)
                svn = sV.n

                def q6_sq3(s, p=p, svn=svn):
                    s.wait_ge(sV.h, svn)
                    sS.inc(s.activation(stt[p][:, 4:5], stt[p][:, 3:4], AF.Sqrt, bias=EPS))
                SC(q6_sq3)
                ssn = sS.n

                wl3q = wl3_hist[tc - 2] if tc >= 2 else 0

                def q6_l3(v, p=p, ssn=ssn, wl3q=wl3q):
                    v.wait_ge(sS.h, ssn)
                    if wl3q:
                        v.wait_ge(wl3[p].h, wl3q)
                    st = stt[p]
                    v.reciprocal(st[:, 5:6], st[:, 4:5])
                    v.tensor_tensor(st[:, 6:7], st[:, 2:3], st[:, 5:6], ALU.mult)
                    v.tensor_scalar(st[:, 6:7], st[:, 6:7], -1.0, None, ALU.mult)
                    sV.inc(v.tensor_scalar(ln3b[p][:, :], x2b[p][:, :], st[:, 5:6], st[:, 6:7],
                                           ALU.mult, ALU.add))
                V(q6_l3)
                svn = sV.n

                def q6_w(s, tc=tc, p=p, svn=svn):
                    s.wait_ge(sV.h, svn)
                    wx2[p].inc(s.dma_start(out=s_x2[tc * 128:(tc + 1) * 128, :], in_=x2b[p][:, :]), 16)
                    wl3[p].inc(s.dma_start(out=s_xln3[tc * 128:(tc + 1) * 128, :], in_=ln3b[p][:, :]), 16)
                SC(q6_w)
                wx2_hist.append(wx2[p].n)
                wl3_hist.append(wl3[p].n)

            s2a = ExitStack()
            with s2a:
                o_sb = sb(s2a, "o_sb", [128, 32 * T], BF16)
                att_f = sb(s2a, "att_f", [128, 4 * T], BF16)
                wo_sb = [sb(s2a, "wo0", [128, 8192], BF16), sb(s2a, "wo1", [128, 8192], BF16)]
                g_wo = [Sem(octx, nc, "gwo0"), Sem(octx, nc, "gwo1")]
                sE6 = Sem(octx, nc, "sE6")
                lo_g = [Sem(octx, nc, f"lo{i}") for i in range(4)]

                def p6_lo(s):
                    s.wait_ge(w_os.h, wos_total)
                    for kc in range(32):
                        lo_g[kc // 8].inc(s.dma_start(out=o_sb[:, kc * T:(kc + 1) * T],
                                                      in_=s_o[kc * 128:(kc + 1) * 128, :]), 16)
                blk.sync(p6_lo)

                wo_free = {}
                g6 = [0]
                ev6_hist = []
                for tt in range(2):
                    for mb in range(16):
                        p = (tt * 16 + mb) % 2
                        stq = wo_free.get(p, 0)

                        def p6_w(s, tt=tt, mb=mb, p=p, stq=stq):
                            if tt == 0 and mb < 2:
                                s.wait_ge(w_os.h, wos_total)
                                s.wait_ge(ag_w["wo"].h, 1)
                            else:
                                s.wait_ge(sT.h, stq)
                            for kcg in range(2):
                                g_wo[p].inc(s.dma_start(
                                    out=wo_sb[p][:, kcg * 4096:(kcg + 1) * 4096]
                                    .rearrange("p (kc j) -> p kc j", j=256),
                                    in_=s_wob[kcg * 2048:(kcg + 1) * 2048, mb * 256:(mb + 1) * 256]
                                    .rearrange("(kc p) j -> p kc j", p=128)), 16)
                        blk.sync(p6_w)

                        for mc2 in range(2):
                            gi = g6[0]
                            myps = PSROT[gi % 4]
                            g6[0] += 1
                            svq = ev6_hist[gi - 4] if gi >= 4 else 0

                            def p6_mm(t, tt=tt, mb=mb, p=p, mc2=mc2, myps=myps, svq=svq):
                                t.wait_ge(g_wo[p].h, g_wo[p].n)
                                for g in lo_g:
                                    t.wait_ge(g.h, g.n)
                                if svq:
                                    t.wait_ge(sE6.h, svq)
                                for kc in range(32):
                                    ins = t.matmul(
                                        myps[:, :],
                                        wo_sb[p][:, kc * 256 + mc2 * 128:kc * 256 + (mc2 + 1) * 128],
                                        o_sb[:, kc * T + tt * 512:kc * T + (tt + 1) * 512],
                                        start=(kc == 0), stop=(kc == 31))
                                    if kc == 31:
                                        sT.inc(ins)
                            blk.tensor(p6_mm)
                            stn = sT.n

                            def p6_ev(v, tt=tt, mb=mb, mc2=mc2, myps=myps, stn=stn):
                                v.wait_ge(sT.h, stn)
                                v.wait_ge(w_att[tt].h, w_att[tt].n)
                                sl = (mb * 2 + mc2) % 4
                                sE6.inc(v.tensor_copy(
                                    att_f[:, sl * T + tt * 512:sl * T + (tt + 1) * 512],
                                    myps[:, :]))
                            V(p6_ev)
                            ev6_hist.append(sE6.n)
                            svn = sE6.n

                            def p6_wr(s, tt=tt, mb=mb, mc2=mc2, svn=svn):
                                s.wait_ge(sE6.h, svn)
                                ch = mb * 2 + mc2
                                sl = ch % 4
                                w_att[tt].inc(s.dma_start(
                                    out=s_att[ch * 128:(ch + 1) * 128, tt * 512:(tt + 1) * 512],
                                    in_=att_f[:, sl * T + tt * 512:sl * T + (tt + 1) * 512]), 16)
                            SC(p6_wr)
                        wo_free[p] = sT.n
                        if tt == 1 and mb in (3, 7, 11, 15):
                            emit_q6_tile((mb - 3) // 4)
                    if tt == 0:
                        q6_state["half0"] = w_att[0].n
                q6_state["half1"] = w_att[1].n
                watt_total = (w_att[0].n, w_att[1].n)

            for tc in (4, 5, 6, 7):
                emit_q6_tile(tc)
            wx2_total = (wx2[0].n, wx2[1].n)
            wl3_total = (wl3[0].n, wl3[1].n)
            s2b.__exit__(None, None, None)

            s2h = ExitStack()
            s2h.__enter__()
            h_feat = sb(s2h, "h_feat", [128, 32 * T], BF16)

            emit_w_chunks(1000)
            s2c = ExitStack()
            with s2c:
                x1s = sb(s2c, "x1s", [128, 4 * T], F32)
                w12t = [sb(s2c, "w12t0", [128, 16384], BF16), sb(s2c, "w12t1", [128, 16384], BF16)]
                xln3T = sb(s2c, "xln3T", [128, 32 * T], BF16)
                twt = [Sem(octx, nc, "twt0"), Sem(octx, nc, "twt1")]

                for hf in range(2):
                    def p7_tr(s, hf=hf):
                        for _i in range(2):
                            s.wait_ge(wl3[_i].h, (wl3_hist[3] if hf == 0 else wl3_total[_i])
                                      if _i == 1 else (wl3_hist[2] if hf == 0 else wl3_total[_i]))
                        for kc in range(32):
                            tr3h[hf].inc(s.dma_start(
                                out=xln3T[:, kc * T + hf * 512:kc * T + hf * 512 + 512],
                                in_=s_xln3[hf * 512:(hf + 1) * 512, kc * 128:(kc + 1) * 128],
                                transpose=True), 16)
                    blk.sync(p7_tr)
                tr3h_tot = (tr3h[0].n, tr3h[1].n)

                w12_free = [0, 0]
                cons7_hist = []
                si_map = {}
                g7 = [0]
                for pr in range(8):
                    for half in range(2):
                        mrow = pr * 512 + half * 4096
                        p = (2 * pr + half) % 2
                        stq = w12_free[p]

                        def p7_w(s, mrow=mrow, p=p, stq=stq, pr=pr, half=half):
                            if pr == 0 and half < 2:
                                s.wait_ge(ag_w["w12"].h, 1)
                                for _i in range(2):
                                    s.wait_ge(wl3[_i].h, wl3_total[_i])
                                    s.wait_ge(wx2[_i].h, wx2_total[_i])
                            if 2 * pr + half >= 2:
                                s.wait_ge(sT.h, stq)
                            r, j0 = mrow // 1024, mrow % 1024
                            for qq in range(4):
                                twt[p].inc(s.dma_start(
                                    out=w12t[p][:, qq * 4096:(qq + 1) * 4096]
                                    .rearrange("p (kc j) -> p kc j", j=512),
                                    in_=s_w12T[r * DM + qq * 1024:r * DM + (qq + 1) * 1024,
                                               j0:j0 + 512]
                                    .rearrange("(kc p) j -> p kc j", p=128)), 16)
                        blk.sync(p7_w)

                        for mc2 in range(4):
                            for tt in range(2):
                                myps = PSROT[(2 * mc2 + tt) % 4]
                                gi = g7[0]
                                g7[0] += 1
                                cons = cons7_hist[gi - 4] if gi >= 4 else None

                                def p7_mm(t, p=p, mc2=mc2, tt=tt, myps=myps, cons=cons):
                                    t.wait_ge(twt[p].h, twt[p].n)
                                    t.wait_ge(tr3h[tt].h, tr3h_tot[tt])
                                    if cons is not None:
                                        eng, cnt = cons
                                        t.wait_ge(sS.h if eng == "S" else sV.h, cnt)
                                    for kc in range(32):
                                        ins = t.matmul(
                                            myps[:, :],
                                            w12t[p][:, kc * 512 + mc2 * 128:kc * 512 + (mc2 + 1) * 128],
                                            xln3T[:, kc * T + tt * 512:kc * T + (tt + 1) * 512],
                                            start=(kc == 0), stop=(kc == 31))
                                        if kc == 31:
                                            sT.inc(ins)
                                blk.tensor(p7_mm)
                                stn = sT.n
                                mcol = (mrow // 128) + mc2

                                if half == 0:
                                    def p7_si(s, mcol=mcol, mc2=mc2, tt=tt, myps=myps, stn=stn):
                                        s.wait_ge(sT.h, stn)
                                        sS.inc(s.activation(
                                            x1s[:, mc2 * T + tt * 512:mc2 * T + (tt + 1) * 512],
                                            myps[:, :], AF.Silu, bias=b12sb[:, mcol:mcol + 1]))
                                    SC(p7_si)
                                    cons7_hist.append(("S", sS.n))
                                    si_map[(pr, mc2, tt)] = sS.n
                                else:
                                    siq = si_map[(pr, mc2, tt)]

                                    def p7_h(v, mcol=mcol, pr=pr, mc2=mc2, tt=tt, myps=myps,
                                             stn=stn, siq=siq):
                                        v.wait_ge(sT.h, stn)
                                        v.wait_ge(sS.h, siq)
                                        ch = pr * 4 + mc2
                                        sV.inc(v.scalar_tensor_tensor(
                                            h_feat[:, ch * T + tt * 512:ch * T + (tt + 1) * 512],
                                            myps[:, :], b12sb[:, 32 + mcol - 32 if half else mcol:
                                                             32 + mcol - 32 + 1 if half else mcol + 1],
                                            x1s[:, mc2 * T + tt * 512:mc2 * T + (tt + 1) * 512],
                                            ALU.add, ALU.mult))
                                    V(p7_h)
                                    cons7_hist.append(("V", sV.n))
                        w12_free[p] = sT.n

            s2e = ExitStack()
            s2e.__enter__()
            trash9 = sb(s2e, "trash9", [128, DM], BF16)
            ftok = [sb(s2e, "ftok0", [128, DM], BF16), sb(s2e, "ftok1", [128, DM], BF16)]
            x2r0 = sb(s2e, "x2r0", [128, DM], F32)
            x2r = [x2r0, x2r0]
            outb = [sb(s2e, "outb0", [128, DM], F32), sb(s2e, "outb1", [128, DM], F32)]
            tft = [Sem(octx, nc, "tft0"), Sem(octx, nc, "tft1")]
            lx3 = [Sem(octx, nc, "lx30"), Sem(octx, nc, "lx31")]
            w_out = [Sem(octx, nc, "w_outa"), Sem(octx, nc, "w_outb")]
            w_ffn = [Sem(octx, nc, "w_ffna"), Sem(octx, nc, "w_ffnb")]
            wout_hist, sv_p9_hist = [], []
            p9_state = {}

            def emit_p9_tile(tc):
                p = tc % 2
                svq = sv_p9_hist[tc - 2] if tc >= 2 else 0

                def p9_ld(s, tc=tc, p=p, svq=svq):
                    if tc == 0:
                        s.wait_ge(w_ffn[0].h, p9_state["half0"])
                        s.wait_ge(wx2[0].h, wx2_total[0])
                        s.wait_ge(wx2[1].h, wx2_total[1])
                    elif tc == 4:
                        s.wait_ge(w_ffn[1].h, p9_state["half1"])
                    if svq:
                        s.wait_ge(sV.h, svq)
                    tft[p].inc(s.dma_start(out=ftok[p][:, :],
                                           in_=s_ffn[:, tc * 128:(tc + 1) * 128],
                                           transpose=True), 16)
                blk.sync(p9_ld)
                xq = sv_p9_hist[tc - 1] if tc >= 1 else 0

                def p9_ldx(s, tc=tc, p=p, xq=xq):
                    if xq:
                        s.wait_ge(sV.h, xq)
                    lx3[p].inc(s.dma_start(out=x2r[p][:, :], in_=s_x2[tc * 128:(tc + 1) * 128, :]), 16)
                blk.sync(p9_ldx)

                def p9_cv(v, p=p):
                    v.wait_ge(tft[p].h, tft[p].n)
                    sV.inc(v.reduce_sum(stt[p][:, 0:1], ftok[p][:, :], axis=mybir.AxisListType.X))
                V(p9_cv)
                svn = sV.n

                def p9_ss(s, p=p, svn=svn):
                    s.wait_ge(sV.h, svn)
                    sS.inc(s.activation(trash9[:, :], ftok[p][:, :], AF.Square, accum_out=stt[p][:, 1:2]))
                SC(p9_ss)
                ssn = sS.n

                def p9_mk(v, p=p, ssn=ssn):
                    v.wait_ge(sS.h, ssn)
                    st = stt[p]
                    v.tensor_scalar(st[:, 2:3], st[:, 0:1], 1.0 / DM, None, ALU.mult)
                    v.tensor_scalar(st[:, 3:4], st[:, 1:2], 1.0 / DM, None, ALU.mult)
                    v.tensor_tensor(st[:, 4:5], st[:, 2:3], st[:, 2:3], ALU.mult)
                    sV.inc(v.tensor_tensor(st[:, 3:4], st[:, 3:4], st[:, 4:5], ALU.subtract))
                V(p9_mk)
                svn = sV.n

                def p9_sq(s, p=p, svn=svn):
                    s.wait_ge(sV.h, svn)
                    sS.inc(s.activation(stt[p][:, 4:5], stt[p][:, 3:4], AF.Sqrt, bias=EPS))
                SC(p9_sq)
                ssn = sS.n

                woq = wout_hist[tc - 2] if tc >= 2 else 0

                def p9_f(v, p=p, ssn=ssn, woq=woq):
                    v.wait_ge(sS.h, ssn)
                    v.wait_ge(lx3[p].h, lx3[p].n)
                    if woq:
                        v.wait_ge(w_out[p].h, woq)
                    st = stt[p]
                    v.reciprocal(st[:, 5:6], st[:, 4:5])
                    v.tensor_tensor(st[:, 6:7], st[:, 2:3], st[:, 5:6], ALU.mult)
                    v.tensor_scalar(st[:, 6:7], st[:, 6:7], -1.0, None, ALU.mult)
                    v.tensor_scalar(outb[p][:, :], ftok[p][:, :], st[:, 5:6], st[:, 6:7],
                                    ALU.mult, ALU.add)
                    sV.inc(v.tensor_tensor(outb[p][:, :], outb[p][:, :], x2r[p][:, :], ALU.add))
                V(p9_f)
                sv_p9_hist.append(sV.n)
                svn = sV.n

                def p9_w(s, tc=tc, p=p, svn=svn):
                    s.wait_ge(sV.h, svn)
                    w_out[p].inc(s.dma_start(out=out[tc * 128:(tc + 1) * 128, :], in_=outb[p][:, :]), 16)
                SC(p9_w)
                wout_hist.append(w_out[p].n)

            s2d = ExitStack()
            with s2d:
                w3t = [sb(s2d, "w3t0", [128, 16384], BF16), sb(s2d, "w3t1", [128, 16384], BF16)]
                ffn_f = sb(s2d, "ffn_f", [128, 4 * T], BF16)
                tw3 = [Sem(octx, nc, "tw30"), Sem(octx, nc, "tw31")]
                sE8 = Sem(octx, nc, "sE8")

                w3_free = {}
                g8 = [0]
                ev8_hist = []
                stq8, svq8 = sT.n, sV.n
                for tt in range(2):
                    for nb in range(8):
                        p = (tt * 8 + nb) % 2
                        stq = w3_free.get(p, 0)

                        def p8_w(s, tt=tt, nb=nb, p=p, stq=stq):
                            if tt == 0 and nb < 2:
                                s.wait_ge(sT.h, stq8)
                                s.wait_ge(sV.h, svq8)
                                s.wait_ge(ag_w["w3"].h, 1)
                            else:
                                s.wait_ge(sT.h, stq)
                            for qq in range(4):
                                tw3[p].inc(s.dma_start(
                                    out=w3t[p][:, qq * 4096:(qq + 1) * 4096]
                                    .rearrange("p (mc j) -> p mc j", j=512),
                                    in_=s_w3T[nb * DM + qq * 1024:nb * DM + (qq + 1) * 1024, :]
                                    .rearrange("(mc p) j -> p mc j", p=128)), 16)
                        blk.sync(p8_w)

                        for nc2 in range(4):
                            gi = g8[0]
                            myps = PSROT[gi % 4]
                            g8[0] += 1
                            svq = ev8_hist[gi - 4] if gi >= 4 else 0

                            def p8_mm(t, tt=tt, nb=nb, p=p, nc2=nc2, myps=myps, svq=svq):
                                t.wait_ge(tw3[p].h, tw3[p].n)
                                if svq:
                                    t.wait_ge(sE8.h, svq)
                                for mc in range(32):
                                    ins = t.matmul(
                                        myps[:, :],
                                        w3t[p][:, mc * 512 + nc2 * 128:mc * 512 + (nc2 + 1) * 128],
                                        h_feat[:, mc * T + tt * 512:mc * T + (tt + 1) * 512],
                                        start=(mc == 0), stop=(mc == 31))
                                    if mc == 31:
                                        sT.inc(ins)
                            blk.tensor(p8_mm)
                            stn = sT.n
                            ncol = nb * 4 + nc2

                            def p8_ev(v, tt=tt, ncol=ncol, nc2=nc2, myps=myps, stn=stn):
                                v.wait_ge(sT.h, stn)
                                v.wait_ge(w_ffn[tt].h, w_ffn[tt].n)
                                sE8.inc(v.tensor_scalar(
                                    ffn_f[:, nc2 * T + tt * 512:nc2 * T + (tt + 1) * 512],
                                    myps[:, :], b3sb[:, ncol:ncol + 1], None, ALU.add))
                            V(p8_ev)
                            ev8_hist.append(sE8.n)
                            svn = sE8.n

                            def p8_wr(s, tt=tt, nb=nb, nc2=nc2, svn=svn):
                                s.wait_ge(sE8.h, svn)
                                ncol = nb * 4 + nc2
                                w_ffn[tt].inc(s.dma_start(
                                    out=s_ffn[ncol * 128:(ncol + 1) * 128, tt * 512:(tt + 1) * 512],
                                    in_=ffn_f[:, nc2 * T + tt * 512:nc2 * T + (tt + 1) * 512]), 16)
                            SC(p8_wr)
                        w3_free[p] = sT.n
                        if tt == 1 and nb in (1, 3, 5, 7):
                            emit_p9_tile((nb - 1) // 2)
                    if tt == 0:
                        p9_state["half0"] = w_ffn[0].n
                p9_state["half1"] = w_ffn[1].n

            for tc in (4, 5, 6, 7):
                emit_p9_tile(tc)
            wout_total = (w_out[0].n, w_out[1].n)

            def p9_fin(s):
                s.wait_ge(w_out[0].h, wout_total[0])
                s.wait_ge(w_out[1].h, wout_total[1])
            blk.sync(p9_fin)
            s2e.__exit__(None, None, None)

            s2h.__exit__(None, None, None)

    return nc


_CACHE = {}


def _consts_shared():
    if "c" in _CACHE:
        return _CACHE["c"]
    r = np.arange(128)
    i64 = np.arange(64)
    c = {}
    c["ident"] = np.eye(128, dtype=BFNP)
    rot = np.zeros((128, 128), np.float32)
    for p in range(128):
        rot[(p + 64) % 128, p] = 1.0
    c["rot"] = rot.astype(BFNP)
    s_ = r[:, None]
    t_ = r[None, :]
    c["lt"] = (((s_ <= t_).astype(np.float32)) / 16.0).astype(BFNP)
    c["uts"] = (((s_ > t_).astype(np.float32)) / 16.0).astype(BFNP)
    c["ones"] = np.ones((128, 1), BFNP)
    c["half"] = np.full((1, 128), 0.5, BFNP)
    c["onerow"] = np.ones((1, 128), BFNP)
    j2 = np.arange(512) // 256
    ii = np.arange(512) % 64
    tau = j2 * 64 + ii
    c["glam"] = ((s_ <= tau[None, :]).astype(np.float32)).astype(BFNP)
    iw = np.tile(np.arange(64), 4)[None, :]
    rr = r[:, None]
    c["swodd"] = (((iw <= rr) & (rr <= iw + 64)).astype(np.float32)).astype(BFNP)
    c["swevA"] = ((rr >= 64 + iw).astype(np.float32)).astype(BFNP)
    c["swevB"] = ((rr <= iw).astype(np.float32)).astype(BFNP)
    _CACHE["c"] = c
    return c


def _core_tables(core):
    key = ("t", core)
    if key in _CACHE:
        return _CACHE[key]
    s0 = (core % 4) * T
    pos = np.arange(s0 - 128, s0 + T).astype(np.float64)
    inv = 10000.0 ** (-np.arange(64, dtype=np.float64) / 64.0)
    fr = np.outer(inv, pos)
    cos = np.concatenate([np.cos(fr), np.cos(fr)], axis=0).astype(BFNP)
    sin = np.concatenate([-np.sin(fr), np.sin(fr)], axis=0).astype(BFNP)
    eoh = np.zeros((128, 8), np.float32)
    eoh[:, core] = 1.0
    cshared = _consts_shared()
    evA1 = np.zeros((128, 256), BFNP) if s0 == 0 else cshared["swevA"]
    _CACHE[key] = (cos, sin, eoh, evA1)
    return _CACHE[key]


def kernel(L0_attn_hidden_states, L0_attn_Wq, L0_attn_Wk, L0_attn_Wv, L0_attn_Wo,
           L0_ffn_w12, L0_ffn_b12, L0_ffn_w3, L0_ffn_b3):
    x = np.asarray(L0_attn_hidden_states)
    Wq = np.ascontiguousarray(L0_attn_Wq)
    Wk = np.ascontiguousarray(L0_attn_Wk)
    Wv = np.ascontiguousarray(L0_attn_Wv)
    Wo = np.ascontiguousarray(L0_attn_Wo)
    w12 = np.ascontiguousarray(L0_ffn_w12)
    w3 = np.ascontiguousarray(L0_ffn_w3)
    b12s = np.ascontiguousarray(np.asarray(L0_ffn_b12).reshape(64, 128).T)
    b3s = np.ascontiguousarray(np.asarray(L0_ffn_b3).reshape(32, 128).T)

    if "nc" not in _CACHE:
        _CACHE["nc"] = build_program()
    nc = _CACHE["nc"]

    cshared = _consts_shared()
    in_maps = []
    for core in range(NCORES):
        b, s0 = core // 4, (core % 4) * T
        cos, sin, eoh, evA1 = _core_tables(core)
        if s0 == 0:
            halo = np.zeros((128, DM), np.float32)
        else:
            halo = np.ascontiguousarray(x[b, s0 - 128:s0])
        m = {
            "x_seg": np.ascontiguousarray(x[b, s0:s0 + T]),
            "x_halo": halo,
            "Wq_sh": np.ascontiguousarray(Wq[core * 512:(core + 1) * 512]),
            "Wk": Wk, "Wv": Wv,
            "Wo_sh": np.ascontiguousarray(Wo[core * 512:(core + 1) * 512]),
            "w12_sh": np.ascontiguousarray(w12[core * 1024:(core + 1) * 1024]),
            "w3_sh": np.ascontiguousarray(w3[core * 512:(core + 1) * 512]),
            "b12s": b12s, "b3s": b3s,
            "ci_ident": cshared["ident"], "ci_rot": cshared["rot"],
            "ci_lt": cshared["lt"], "ci_uts": cshared["uts"],
            "ci_ones": cshared["ones"], "ci_half": cshared["half"],
            "ci_onerow": cshared["onerow"], "ci_glam": cshared["glam"],
            "ci_swodd": cshared["swodd"], "ci_swevA": cshared["swevA"],
            "ci_swevB": cshared["swevB"], "ci_swevA1": evA1,
            "ci_cos": cos, "ci_sin": sin, "ci_eoh": eoh,
            "cvals_f": np.tile(np.array([[EPS, SHIFT, ISQD, -1.0, 1.0 / DM, 0.5, 1.0, 0.0]],
                                        np.float32), (128, 1)),
            "cvals_b": np.tile(np.array([[SHIFT, -1.0, 1.0, 0.0]], np.float32).astype(BFNP),
                               (128, 1)),
        }
        in_maps.append(m)

    res = run_bass_kernel_spmd(nc, in_maps, core_ids=list(range(NCORES)))
    outp = np.empty((2, 4096, DM), np.float32)
    for core in range(NCORES):
        b, s0 = core // 4, (core % 4) * T
        outp[b, s0:s0 + T] = res.results[core]["out"]
    return outp
